# version 02
# baseline (speedup 1.0000x reference)
import contextlib
import numpy as np
import ml_dtypes
import concourse.bass as bass
import concourse.mybir as mybir
from concourse.bass_utils import run_bass_kernel_spmd

F32 = mybir.dt.float32
BF16 = mybir.dt.bfloat16
I32 = mybir.dt.int32
AF = mybir.ActivationFunctionType
ALU = mybir.AluOpType
AX = mybir.AxisListType

NCORES = 8
B, T, D = 4, 8192, 1024
TPC = 4096
TC = 512
NCH = TPC // TC
ND = D // 128
NS = 16
DTR = 64
NH, DH, WIN = 16, 64, 512
H0 = 3072
H1 = 4096
EPS = 1e-5
QSCALE = 1.0 / 127.0
NO0 = H0 // 128
NO1 = H1 // 128
TCE = 1024
NCHE = TPC // TCE
ENGS = ("sync", "tensor", "vector", "scalar", "gpsimd")
PAIRS = [[0, 1], [2, 3], [4, 5], [6, 7]]


class Prog:
    def __init__(self, nc):
        self.nc = nc
        self.items = []
        self.cnt = {}
        self.pending_bar = set()
        self.bar_target = 0

    def E(self, eng, fn, waits=(), inc=None, amt=1):
        w = {}
        if eng in self.pending_bar:
            w["ph"] = self.bar_target
            self.pending_bar.discard(eng)
        for (s, v) in waits:
            if v > 0:
                w[s] = max(w.get(s, 0), v)
        if inc is not None:
            self.cnt[inc] = self.cnt.get(inc, 0) + amt
        self.items.append((eng, fn, sorted(w.items()), inc, amt))
        return self.cnt.get(inc, 0) if inc else 0

    def c(self, name):
        return self.cnt.get(name, 0)

    def barrier(self, extra_sync_waits=()):
        self.E("sync", lambda eng: eng.nop(), waits=list(extra_sync_waits), inc="ph")
        for e in ("tensor", "vector", "scalar", "gpsimd"):
            self.E(e, lambda eng: eng.nop(), inc="ph")
        self.bar_target = self.c("ph")
        self.pending_bar = set(ENGS)

    def emit(self):
        nc = self.nc
        names = sorted({n for it in self.items for (n, _) in it[2]}
                       | {it[3] for it in self.items if it[3]})
        sems = {n: nc.alloc_semaphore(name="s_" + n) for n in names}
        with nc.Block() as block:
            for eng_name in ENGS:
                def body(eng, _en=eng_name):
                    for (e, fn, waits, inc, amt) in self.items:
                        if e != _en:
                            continue
                        for (s, v) in waits:
                            eng.wait_ge(sems[s], v)
                        ins = fn(eng) if fn is not None else None
                        if inc is not None:
                            ins.then_inc(sems[inc], amt)
                getattr(block, eng_name)(body)


def emit_layernorm(p, src_tiles, xlnb, ps, rows, sq_waits, bc_waits, xln_waits,
                   xln32_tiles=None, tc=TC):
    nd = len(src_tiles)
    onec, oner = rows["onec"], rows["oner"]
    x2 = rows["x2"]

    def sx_fn(eng):
        last = None
        for k, ap in enumerate(src_tiles):
            last = eng.matmul(ps["sx"][:, :], onec[:, :], ap, start=(k == 0), stop=(k == nd - 1))
        return last
    pe_st = p.E("tensor", sx_fn,
                waits=list(sq_waits) + [("ve", getattr(p, "ln_r1_prev", 0))], inc="pe")
    if not hasattr(p, "ln_x2_free"):
        p.ln_x2_free = 0
    for k, ap in enumerate(src_tiles):
        sq_done = p.E("scalar", lambda eng, ap=ap: eng.activation(
            out=x2[:, 0:tc], in_=ap, func=AF.Square),
            waits=list(sq_waits) + [("pe", p.ln_x2_free)], inc="ac")
        pe_st = p.E("tensor", lambda eng, k=k: eng.matmul(
            ps["sx2"][:, :], onec[:, :], x2[:, 0:tc],
            start=(k == 0), stop=(k == nd - 1)),
            waits=[("ac", sq_done)], inc="pe")
        p.ln_x2_free = pe_st

    def r1_fn(eng):
        eng.tensor_scalar_mul(out=rows["mean"][:, :], in0=ps["sx"][:, :], scalar1=1.0 / D)
        eng.tensor_scalar_mul(out=rows["ta"][:, :], in0=ps["sx2"][:, :], scalar1=1.0 / D)
        eng.tensor_mul(out=rows["tb"][:, :], in0=rows["mean"][:, :], in1=rows["mean"][:, :])
        eng.tensor_sub(out=rows["ta"][:, :], in0=rows["ta"][:, :], in1=rows["tb"][:, :])
        return eng.tensor_scalar_add(out=rows["ta"][:, :], in0=rows["ta"][:, :], scalar1=EPS)
    dve_r1 = p.E("vector", r1_fn, waits=[("pe", pe_st)], inc="ve")

    a_sqr = p.E("scalar", lambda eng: eng.activation(
        out=rows["tb"][:, :], in_=rows["ta"][:, :], func=AF.Sqrt),
        waits=[("ve", dve_r1)], inc="ac")

    def r2_fn(eng):
        eng.reciprocal(out=rows["rstd"][:, :], in_=rows["tb"][:, :])
        return eng.tensor_mul(out=rows["mr"][:, :], in0=rows["mean"][:, :], in1=rows["rstd"][:, :])
    dve_r2 = p.E("vector", r2_fn, waits=[("ac", a_sqr)], inc="ve")
    p.ln_r1_prev = dve_r1

    def bc_fn(eng):
        eng.matmul(ps["b1"][:, :], oner[:, :], rows["rstd"][:, :], start=True, stop=True)
        return eng.matmul(ps["b2"][:, :], oner[:, :], rows["mr"][:, :], start=True, stop=True)
    pe_bc = p.E("tensor", bc_fn, waits=[("ve", dve_r2)] + list(bc_waits), inc="pe")

    tmp0 = rows["tmp0"]

    def xln_fn(eng):
        last = None
        for k, ap in enumerate(src_tiles):
            eng.tensor_tensor(out=tmp0[:, :], in0=ap, in1=ps["b1"][:, :], op=ALU.mult)
            dst32 = xln32_tiles[k] if xln32_tiles is not None else tmp0[:, :]
            eng.tensor_tensor(out=dst32, in0=tmp0[:, :], in1=ps["b2"][:, :], op=ALU.subtract)
            last = eng.tensor_copy(out=xlnb[k], in_=dst32)
        return last
    dve_x = p.E("vector", xln_fn, waits=[("pe", pe_bc)] + list(xln_waits), inc="ve")
    return {"dve_xln": dve_x, "pe_bc": pe_bc, "pe_st": pe_st}


def build_fused():
    nc = bass.Bass("TRN2", target_bir_lowering=False, debug=False)
    p = Prog(nc)

    xT = nc.dram_tensor("xT", [D, TPC], F32, kind="ExternalInput").ap()
    wbT = nc.dram_tensor("wbT", [D, NS], F32, kind="ExternalInput").ap()
    wcT = nc.dram_tensor("wcT", [D, NS], F32, kind="ExternalInput").ap()
    wdt1T = nc.dram_tensor("wdt1T", [D, DTR], F32, kind="ExternalInput").ap()
    wdt2T = nc.dram_tensor("wdt2T", [DTR, ND, 128], F32, kind="ExternalInput").ap()
    bdt2c = nc.dram_tensor("bdt2c", [128, ND], F32, kind="ExternalInput").ap()
    logA_pn = nc.dram_tensor("logA_pn", [128, ND * NS], F32, kind="ExternalInput").ap()
    logAT = nc.dram_tensor("logAT", [NS, D], F32, kind="ExternalInput").ap()
    eye16_d = nc.dram_tensor("eye16", [NS, NS], F32, kind="ExternalInput").ap()
    dskc = nc.dram_tensor("dskc", [128, ND], F32, kind="ExternalInput").ap()
    flag_d = nc.dram_tensor("flag", [128, 1], F32, kind="ExternalInput").ap()
    wg_l = nc.dram_tensor("wg_l", [NO0, 128, D], I32, kind="ExternalInput").ap()
    wu_l = nc.dram_tensor("wu_l", [NO0, 128, D], I32, kind="ExternalInput").ap()
    wd_l = nc.dram_tensor("wd_l", [ND, 128, H0], I32, kind="ExternalInput").ap()
    wqT = nc.dram_tensor("wqT", [D, D], F32, kind="ExternalInput").ap()
    wkT = nc.dram_tensor("wkT", [D, D], F32, kind="ExternalInput").ap()
    wvT = nc.dram_tensor("wvT", [D, D], F32, kind="ExternalInput").ap()
    woT = nc.dram_tensor("woT", [D, D], F32, kind="ExternalInput").ap()
    mask_m = nc.dram_tensor("mask_m", [4, 128, 2 * WIN], F32, kind="ExternalInput").ap()
    mask_0 = nc.dram_tensor("mask_0", [4, 128, 2 * WIN], F32, kind="ExternalInput").ap()
    ident_d = nc.dram_tensor("ident", [128, 128], BF16, kind="ExternalInput").ap()
    w1_l = nc.dram_tensor("w1_l", [NO1, 128, D], F32, kind="ExternalInput").ap()
    w2_l = nc.dram_tensor("w2_l", [NO1, 128, D], F32, kind="ExternalInput").ap()
    h1_l = nc.dram_tensor("h1_l", [NO1, 128, H1], F32, kind="ExternalInput").ap()
    h2_l = nc.dram_tensor("h2_l", [NO1, 128, H1], F32, kind="ExternalInput").ap()
    w3_l = nc.dram_tensor("w3_l", [ND, 128, H1], F32, kind="ExternalInput").ap()
    scd = {n: nc.dram_tensor(n, [128, NO1 if n != "s3c" else ND], F32, kind="ExternalInput").ap()
           for n in ("s1c", "s2c", "sh1c", "sh2c", "s3c")}
    onec_d = nc.dram_tensor("ones_col", [128, 1], F32, kind="ExternalInput").ap()
    oner_d = nc.dram_tensor("ones_row", [1, 128], F32, kind="ExternalInput").ap()
    out_o = nc.dram_tensor("outT", [D, TPC], F32, kind="ExternalOutput").ap()

    r1d = nc.dram_tensor("r1d", [D, TPC], F32)
    scumd = nc.dram_tensor("scumd", [D, TPC], F32)
    ctd = nc.dram_tensor("ctd", [NS, TPC], F32)
    r2d = nc.dram_tensor("r2d", [D, TPC], F32)
    r3d = nc.dram_tensor("r3d", [D, TPC], F32)
    hfin_src = nc.dram_tensor("hfin_src", [128, ND * NS], F32)
    hfin_g = nc.dram_tensor("hfin_g", [256, ND * NS], F32)
    halo_src = nc.dram_tensor("halo_src", [D, WIN], F32)
    halo_g = nc.dram_tensor("halo_g", [2 * D, WIN], F32)

    xT_r = xT.rearrange("(k p) t -> p k t", p=128)
    r1_rr = r1d.ap().rearrange("(k p) t -> p k t", p=128)
    scum_rr = scumd.ap().rearrange("(k p) t -> p k t", p=128)
    r2_rr = r2d.ap().rearrange("(k p) t -> p k t", p=128)
    r3_rr = r3d.ap().rearrange("(k p) t -> p k t", p=128)
    halo_rr = halo_g.ap()[0:D, :].rearrange("(k p) t -> p k t", p=128)
    out_rr = out_o.rearrange("(k p) t -> p k t", p=128)

    rows = {n: nc.alloc_sbuf_tensor("r_" + n, [1, TC], F32)
            for n in ("mean", "ta", "tb", "rstd", "mr")}
    rows["onec"] = nc.alloc_sbuf_tensor("onec", [128, 1], F32)
    rows["oner"] = nc.alloc_sbuf_tensor("oner", [1, 128], F32)
    rows["x2"] = nc.alloc_sbuf_tensor("x2", [128, TC], F32)
    rows["tmp0"] = nc.alloc_sbuf_tensor("tmp0", [128, TC], F32)
    ps = {
        "sx": nc.alloc_psum_tensor("ps_sx", [1, TC], F32),
        "sx2": nc.alloc_psum_tensor("ps_sx2", [1, TC], F32),
        "b1": nc.alloc_psum_tensor("ps_b1", [128, TC], F32),
        "b2": nc.alloc_psum_tensor("ps_b2", [128, TC], F32),
    }
    A_pn = nc.alloc_sbuf_tensor("A_pn", [128, ND * NS], F32)
    SA = nc.alloc_sbuf_tensor("SA", [NS, D], F32)
    ones16 = nc.alloc_sbuf_tensor("ones16", [NS, 128], F32)
    eye16 = nc.alloc_sbuf_tensor("eye16s", [NS, NS], F32)
    dsk = nc.alloc_sbuf_tensor("dsk", [128, ND], F32)
    flag_sb = nc.alloc_sbuf_tensor("flag_sb", [128, 1], F32)
    h_in = nc.alloc_sbuf_tensor("h_in", [128, ND * NS], F32)
    carry = nc.alloc_sbuf_tensor("carry", [128, ND * NS], F32)
    carry_s = nc.alloc_sbuf_tensor("carry_s", [128, ND], F32)

    dc = 0
    for (dst, src) in ((rows["onec"], onec_d), (rows["oner"], oner_d),
                       (A_pn, logA_pn), (SA, logAT), (eye16, eye16_d),
                       (dsk, dskc), (flag_sb, flag_d)):
        dc = p.E("sync", lambda eng, dst=dst, src=src: eng.dma_start(
            out=dst[:, :], in_=src), inc="dx", amt=16)
    apl = p.E("scalar", lambda eng: (
        eng.activation(out=A_pn[:, :], in_=A_pn[:, :], func=AF.Exp),
        eng.activation(out=SA[:, :], in_=SA[:, :], func=AF.Exp))[-1],
        waits=[("dx", dc)], inc="ac")

    def pl_v(eng):
        eng.tensor_scalar_mul(out=A_pn[:, :], in0=A_pn[:, :], scalar1=-1.0)
        eng.tensor_scalar(out=SA[:, :], in0=SA[:, :], scalar1=-1.0, scalar2=1e-8,
                          op0=ALU.mult, op1=ALU.add)
        eng.reciprocal(out=SA[:, :], in_=SA[:, :])
        eng.memset(ones16[:, :], 1.0)
        eng.memset(carry[:, :], 0.0)
        return eng.memset(carry_s[:, :], 0.0)
    vpl = p.E("vector", pl_v, waits=[("ac", apl)], inc="ve")

    with contextlib.ExitStack() as st:
        wbl = st.enter_context(nc.sbuf_tensor("wbl", [128, ND, NS], BF16))
        wcl = st.enter_context(nc.sbuf_tensor("wcl", [128, ND, NS], BF16))
        wdt1l = st.enter_context(nc.sbuf_tensor("wdt1l", [128, ND, DTR], BF16))
        wdt2l = st.enter_context(nc.sbuf_tensor("wdt2l", [DTR, ND, 128], BF16))
        bdt2 = st.enter_context(nc.sbuf_tensor("bdt2", [128, ND], F32))
        xs = [st.enter_context(nc.sbuf_tensor(f"xs{i}", [128, ND, TC], F32)) for i in range(2)]
        xlns = [st.enter_context(nc.sbuf_tensor(f"xlns{i}", [128, ND, TC], F32)) for i in range(2)]
        xlnb = [st.enter_context(nc.sbuf_tensor(f"xlnb{i}", [128, ND, TC], BF16)) for i in range(2)]
        dts = st.enter_context(nc.sbuf_tensor("dts", [128, ND, TC], F32))
        scums = st.enter_context(nc.sbuf_tensor("scums", [128, ND, TC], F32))
        y_sb = [st.enter_context(nc.sbuf_tensor(f"y_sb{i}", [128, ND, TC], F32)) for i in range(2)]
        bt_sb = st.enter_context(nc.sbuf_tensor("bt_sb", [NS, TC], F32))
        ct_sb = st.enter_context(nc.sbuf_tensor("ct_sb", [NS, TC], F32))
        t1b = st.enter_context(nc.sbuf_tensor("t1b", [DTR, TC], BF16))
        l_sp = st.enter_context(nc.sbuf_tensor("l_sp", [128, TC], F32))
        r_sp = st.enter_context(nc.sbuf_tensor("r_sp", [128, TC], F32))
        onesT = st.enter_context(nc.sbuf_tensor("onesT", [128, TC], F32))
        abar = [st.enter_context(nc.sbuf_tensor(f"abar{i}", [128, TC], F32)) for i in range(2)]
        t2 = st.enter_context(nc.sbuf_tensor("t2", [128, TC], F32))
        b_t = st.enter_context(nc.sbuf_tensor("b_t", [128, TC], F32))
        h_t = st.enter_context(nc.sbuf_tensor("h_t", [128, TC], F32))
        tmpy = st.enter_context(nc.sbuf_tensor("tmpy", [128, TC], F32))
        ps_m = [st.enter_context(nc.psum_tensor(f"ps_m{i}", [128, TC], F32)) for i in range(2)]
        ps_rab = [st.enter_context(nc.psum_tensor(f"ps_rab{i}", [128, TC], F32)) for i in range(2)]

        dw = 0
        for (dst, src) in ((wbl, wbT.rearrange("(k p) n -> p k n", p=128)),
                           (wcl, wcT.rearrange("(k p) n -> p k n", p=128)),
                           (wdt1l, wdt1T.rearrange("(k p) n -> p k n", p=128))):
            dw = p.E("gpsimd", lambda eng, dst=dst, src=src: eng.dma_start(
                out=dst[:, :, :], in_=src), inc="dw", amt=16)
        dw = p.E("gpsimd", lambda eng: eng.dma_start(out=wdt2l[:, :, :], in_=wdt2T),
                 inc="dw", amt=16)
        dcb = p.E("sync", lambda eng: eng.dma_start(out=bdt2[:, :], in_=bdt2c),
                  inc="dx", amt=16)
        vones = p.E("vector", lambda eng: eng.memset(onesT[:, :], 1.0), inc="ve")

        Bm_t = st.enter_context(nc.sbuf_tensor("Bm_t", [NS, TC], F32))
        Cm_t = st.enter_context(nc.sbuf_tensor("Cm_t", [NS, TC], F32))

        hist = {0: {}, 1: {}}
        last_a_sp = 0
        last_dve_dt = 0
        last_abar_read = 0
        rab_hist = [0, 0]
        cb_hist = [0, 0]
        prev_n_pe = 0
        abar_free = [0, 0]
        scum_store = 0
        chunk_done_ve = 0
        ct_stored = 0
        for c in range(NCH):
            par = c % 2
            c0 = c * TC
            h = hist[par]
            xv = p.E("sync", lambda eng, par=par, c0=c0: eng.dma_start(
                out=xs[par][:, :, :], in_=xT_r[:, :, c0:c0 + TC]),
                waits=[("ve", h.get("x_consumed", 0)), ("do", h.get("do", 0))],
                inc="dx", amt=16)
            lnr = emit_layernorm(
                p, [xs[par][:, k, :] for k in range(ND)],
                [xlnb[par][:, k, :] for k in range(ND)], ps, rows,
                sq_waits=[("dx", xv)],
                bc_waits=[("ve", hist.get("dve_xln_prev", 0))],
                xln_waits=[("pe", h.get("pe_pr", 0)), ("ve", h.get("xln_consumed", 0))],
                xln32_tiles=[xlns[par][:, k, :] for k in range(ND)])
            hist["dve_xln_prev"] = lnr["dve_xln"]

            def bt_fn(eng, par=par):
                last = None
                for k in range(ND):
                    last = eng.matmul(ps_m[0][0:NS, :], wbl[:, k, :],
                                      xlnb[par][:, k, :], start=(k == 0), stop=(k == ND - 1))
                return last
            pe_bt = p.E("tensor", bt_fn,
                        waits=[("ve", lnr["dve_xln"]), ("dw", dw),
                               ("ve", chunk_done_ve)],
                        inc="pe")
            ve_bt = p.E("vector", lambda eng: eng.tensor_copy(
                out=bt_sb[:, :], in_=ps_m[0][0:NS, :]),
                waits=[("pe", pe_bt)], inc="ve")

            def ct_fn(eng, par=par):
                last = None
                for k in range(ND):
                    last = eng.matmul(ps_m[1][0:NS, :], wcl[:, k, :],
                                      xlnb[par][:, k, :], start=(k == 0), stop=(k == ND - 1))
                return last
            pe_ct = p.E("tensor", ct_fn, waits=[("ve", chunk_done_ve)], inc="pe")
            ve_ct = p.E("vector", lambda eng: eng.tensor_copy(
                out=ct_sb[:, :], in_=ps_m[1][0:NS, :]),
                waits=[("pe", pe_ct), ("do", ct_stored)], inc="ve")
            ct_stored = p.E("sync", lambda eng, c0=c0: eng.dma_start(
                out=ctd.ap()[:, c0:c0 + TC], in_=ct_sb[:, :]),
                waits=[("ve", ve_ct)], inc="do", amt=16)

            def t1_fn(eng, par=par):
                last = None
                for k in range(ND):
                    last = eng.matmul(ps_m[0][0:DTR, :], wdt1l[:, k, :],
                                      xlnb[par][:, k, :], start=(k == 0), stop=(k == ND - 1))
                return last
            pe_t1 = p.E("tensor", t1_fn, waits=[("ve", ve_bt)], inc="pe")
            a_t1c = p.E("scalar", lambda eng: eng.copy(out=t1b[:, :], in_=ps_m[0][0:DTR, :]),
                        waits=[("pe", pe_t1)], inc="ac")
            h["pe_pr"] = pe_t1

            for j in range(ND):
                mi = j % 2
                pe_dt = p.E("tensor", lambda eng, j=j, mi=mi: eng.matmul(
                    ps_m[mi][:, :], wdt2l[:, j, :], t1b[:, :], start=True, stop=True),
                    waits=[("ac", a_t1c), ("ac", last_a_sp),
                           ("ve", ve_ct if mi == 1 else 0)], inc="pe")

                def sp_fn(eng, j=j, mi=mi):
                    eng.activation(out=l_sp[:, :], in_=ps_m[mi][:, :], func=AF.Abs, bias=bdt2[:, j:j + 1])
                    eng.activation(out=l_sp[:, :], in_=l_sp[:, :], func=AF.Exp, scale=-1.0)
                    eng.activation(out=l_sp[:, :], in_=l_sp[:, :], func=AF.Ln, bias=1.0)
                    return eng.activation(out=r_sp[:, :], in_=ps_m[mi][:, :], func=AF.Relu, bias=bdt2[:, j:j + 1])
                last_a_sp = p.E("scalar", sp_fn,
                                waits=[("pe", pe_dt), ("ve", last_dve_dt), ("dx", dcb)], inc="ac")
                last_dve_dt = p.E("vector", lambda eng, j=j: eng.tensor_add(
                    out=dts[:, j, :], in0=l_sp[:, :], in1=r_sp[:, :]),
                    waits=[("ac", last_a_sp), ("ac", last_abar_read),
                           ("do", scum_store)], inc="ve")
                sc_scan = p.E("vector", lambda eng, j=j: eng.tensor_tensor_scan(
                    out=scums[:, j, :], data0=onesT[:, :], data1=dts[:, j, :],
                    initial=carry_s[:, j:j + 1], op0=ALU.mult, op1=ALU.add),
                    waits=[("ve", vones), ("ve", vpl)], inc="ve")
                last_dve_dt = p.E("vector", lambda eng, j=j: eng.tensor_copy(
                    out=carry_s[:, j:j + 1], in_=scums[:, j, TC - 1:TC]),
                    waits=[("ve", sc_scan)], inc="ve")
            scum_store = p.E("sync", lambda eng, c0=c0: eng.dma_start(
                out=scum_rr[:, :, c0:c0 + TC], in_=scums[:, :, :]),
                waits=[("ve", last_dve_dt)], inc="do", amt=16)

            ymem = p.E("vector", lambda eng, par=par: eng.memset(y_sb[par][:, :, :], 0.0),
                       waits=[("do", h.get("do", 0)), ("ve", vpl)], inc="ve")
            for n in range(NS):
                dve_m = p.E("vector", lambda eng, n=n: (
                    eng.tensor_scalar_mul(out=Bm_t[:, :], in0=bt_sb[:, :], scalar1=eye16[:, n:n + 1]),
                    eng.tensor_scalar_mul(out=Cm_t[:, :], in0=ct_sb[:, :], scalar1=eye16[:, n:n + 1]))[-1],
                    waits=[("pe", prev_n_pe)], inc="ve")
                cbi = n % 2
                pe_cb = p.E("tensor", lambda eng, cbi=cbi: eng.matmul(
                    ps_m[cbi][:, :], ones16[:, :], Cm_t[:, :], start=True, stop=True),
                    waits=[("ve", dve_m), ("ve", cb_hist[cbi]),
                           ("ac", last_a_sp)], inc="pe")
                cb_reads = 0
                for d in range(ND):
                    ri = (n * ND + d) % 2
                    pe_rab = p.E("tensor", lambda eng, ri=ri, d=d: eng.matmul(
                        ps_rab[ri][:, :], SA[:, d * 128:(d + 1) * 128], Bm_t[0:NS, :],
                        start=True, stop=True),
                        waits=[("ve", dve_m), ("ve", rab_hist[ri])], inc="pe")
                    prev_n_pe = pe_rab
                    ai = (n * ND + d) % 2
                    dn = d * NS + n
                    a_ab = p.E("scalar", lambda eng, d=d, dn=dn, ai=ai: eng.activation(
                        out=abar[ai][:, :], in_=dts[:, d, :], func=AF.Exp,
                        scale=A_pn[:, dn:dn + 1]),
                        waits=[("ve", abar_free[ai]), ("ve", last_dve_dt)], inc="ac")
                    last_abar_read = a_ab
                    dve_t2 = p.E("vector", lambda eng, par=par, d=d, ri=ri: eng.tensor_tensor(
                        out=t2[:, :], in0=xlns[par][:, d, :], in1=ps_rab[ri][:, :], op=ALU.mult),
                        waits=[("pe", pe_rab)], inc="ve")
                    rab_hist[ri] = dve_t2
                    p.E("vector", lambda eng, ai=ai: eng.scalar_tensor_tensor(
                        out=b_t[:, :], in0=abar[ai][:, :], scalar=-1.0, in1=t2[:, :],
                        op0=ALU.add, op1=ALU.mult),
                        waits=[("ac", a_ab)], inc="ve")
                    dve_h = p.E("vector", lambda eng, dn=dn, ai=ai: eng.tensor_tensor_scan(
                        out=h_t[:, :], data0=abar[ai][:, :], data1=b_t[:, :],
                        initial=carry[:, dn:dn + 1], op0=ALU.mult, op1=ALU.add),
                        inc="ve")
                    abar_free[ai] = dve_h
                    p.E("vector", lambda eng, dn=dn: eng.tensor_copy(
                        out=carry[:, dn:dn + 1], in_=h_t[:, TC - 1:TC]),
                        waits=[("ve", dve_h)], inc="ve")
                    cb_reads = p.E("vector", lambda eng, cbi=cbi: eng.tensor_tensor(
                        out=tmpy[:, :], in0=h_t[:, :], in1=ps_m[cbi][:, :], op=ALU.mult),
                        waits=[("pe", pe_cb), ("ve", ymem)], inc="ve")
                    p.E("vector", lambda eng, par=par, d=d: eng.tensor_tensor(
                        out=y_sb[par][:, d, :], in0=y_sb[par][:, d, :], in1=tmpy[:, :], op=ALU.add),
                        inc="ve")
                cb_hist[cbi] = cb_reads
            ve_r1 = 0
            for k in range(ND):
                p.E("vector", lambda eng, par=par, k=k: eng.scalar_tensor_tensor(
                    out=y_sb[par][:, k, :], in0=xlns[par][:, k, :], scalar=dsk[:, k:k + 1],
                    in1=y_sb[par][:, k, :], op0=ALU.mult, op1=ALU.add),
                    waits=[("dx", dc)], inc="ve")
                ve_r1 = p.E("vector", lambda eng, par=par, k=k: eng.tensor_add(
                    out=y_sb[par][:, k, :], in0=y_sb[par][:, k, :], in1=xs[par][:, k, :]),
                    inc="ve")
            do = p.E("sync", lambda eng, par=par, c0=c0: eng.dma_start(
                out=r1_rr[:, :, c0:c0 + TC], in_=y_sb[par][:, :, :]),
                waits=[("ve", ve_r1)], inc="do", amt=16)
            h.update(do=do, x_consumed=ve_r1, xln_consumed=ve_r1)
            chunk_done_ve = p.c("ve")

        sfin = p.E("sync", lambda eng: eng.dma_start(
            out=hfin_src.ap(), in_=carry[:, :]),
            waits=[("ve", p.c("ve"))], inc="do", amt=16)
        cc1 = p.E("gpsimd", lambda eng: eng.collective_compute(
            "AllGather", ALU.bypass, replica_groups=PAIRS,
            ins=[hfin_src.ap().opt()], outs=[hfin_g.ap().opt()]),
            waits=[("do", sfin)], inc="cc", amt=1)
        dhin = p.E("sync", lambda eng: eng.dma_start(
            out=h_in[:, :], in_=hfin_g.ap()[0:128, :]),
            waits=[("cc", cc1)], inc="dx", amt=16)
        vhin = p.E("vector", lambda eng: eng.tensor_scalar_mul(
            out=h_in[:, :], in0=h_in[:, :], scalar1=flag_sb[:, :]),
            waits=[("dx", dhin), ("dx", dc)], inc="ve")

    p.barrier(extra_sync_waits=[("do", p.c("do")), ("dw", p.c("dw"))])
    with contextlib.ExitStack() as st:
        wgl_s = [st.enter_context(nc.sbuf_tensor(f"wgl_s{i}", [128, D], BF16)) for i in range(2)]
        wul_s = [st.enter_context(nc.sbuf_tensor(f"wul_s{i}", [128, D], BF16)) for i in range(2)]
        wd_sb = [st.enter_context(nc.sbuf_tensor(f"wd_sb{i}", [128, H0], BF16)) for i in range(2)]
        r1_sb = st.enter_context(nc.sbuf_tensor("r1_sb", [128, ND, TC], F32))
        scum_sb = st.enter_context(nc.sbuf_tensor("scum_sb", [128, ND, TC], F32))
        cts3 = st.enter_context(nc.sbuf_tensor("cts3", [NS, TC], F32))
        lnb3v = st.enter_context(nc.sbuf_tensor("lnb3v3", [128, ND, TC], BF16))
        hb = st.enter_context(nc.sbuf_tensor("hb", [128, NO0, TC], BF16))
        sg = st.enter_context(nc.sbuf_tensor("sg", [128, TC], F32))
        ebuf = [st.enter_context(nc.sbuf_tensor(f"ebuf{i}", [128, TC], F32)) for i in range(2)]
        e2 = st.enter_context(nc.sbuf_tensor("e2", [128, TC], F32))
        t3 = st.enter_context(nc.sbuf_tensor("t3", [128, TC], F32))
        Cm3 = st.enter_context(nc.sbuf_tensor("Cm3", [NS, TC], F32))
        ps_p = [st.enter_context(nc.psum_tensor(f"ps_p{i}", [128, TC], F32)) for i in range(3)]

        h = {}
        gu_used = [0, 0]
        wd_used = [0, 0]
        cb3_hist = [0, 0]
        ebuf_free = [0, 0]
        prev_pe3 = 0
        for c in range(NCH):
            c0 = c * TC
            dl1 = p.E("sync", lambda eng, c0=c0: eng.dma_start(
                out=r1_sb[:, :, :], in_=r1_rr[:, :, c0:c0 + TC]),
                waits=[("do", h.get("do", 0)), ("ve", h.get("r1_free", 0))], inc="dx", amt=16)
            dl2 = p.E("sync", lambda eng, c0=c0: eng.dma_start(
                out=scum_sb[:, :, :], in_=scum_rr[:, :, c0:c0 + TC]),
                waits=[("ac", h.get("scum_free", 0))], inc="dx", amt=16)
            dl3 = p.E("sync", lambda eng, c0=c0: eng.dma_start(
                out=cts3[:, :], in_=ctd.ap()[:, c0:c0 + TC]),
                waits=[("ve", h.get("ct_free", 0))], inc="dx", amt=16)
            ve_corr = 0
            for n in range(NS):
                dve_m = p.E("vector", lambda eng, n=n: eng.tensor_scalar_mul(
                    out=Cm3[:, :], in0=cts3[:, :], scalar1=eye16[:, n:n + 1]),
                    waits=[("dx", dl3), ("pe", prev_pe3), ("ve", vhin)], inc="ve")
                cbi = n % 2
                pe_cb = p.E("tensor", lambda eng, cbi=cbi: eng.matmul(
                    ps_p[cbi][:, :], ones16[:, :], Cm3[:, :], start=True, stop=True),
                    waits=[("ve", dve_m), ("ve", cb3_hist[cbi]),
                           ("ac", h.get("ac_epi", 0)), ("ve", h.get("dve_h_mult", 0))],
                    inc="pe")
                prev_pe3 = pe_cb
                cb_reads = 0
                for d in range(ND):
                    ai = (n * ND + d) % 2
                    dn = d * NS + n
                    a_e = p.E("scalar", lambda eng, d=d, dn=dn, ai=ai: eng.activation(
                        out=ebuf[ai][:, :], in_=scum_sb[:, d, :], func=AF.Exp,
                        scale=A_pn[:, dn:dn + 1]),
                        waits=[("dx", dl2), ("ve", ebuf_free[ai])], inc="ac")
                    ve_e2 = p.E("vector", lambda eng, dn=dn, ai=ai: eng.tensor_scalar_mul(
                        out=e2[:, :], in0=ebuf[ai][:, :], scalar1=h_in[:, dn:dn + 1]),
                        waits=[("ac", a_e)], inc="ve")
                    ebuf_free[ai] = ve_e2
                    cb_reads = p.E("vector", lambda eng, cbi=cbi: eng.tensor_tensor(
                        out=t3[:, :], in0=e2[:, :], in1=ps_p[cbi][:, :], op=ALU.mult),
                        waits=[("pe", pe_cb), ("dx", dl1)], inc="ve")
                    ve_corr = p.E("vector", lambda eng, d=d: eng.tensor_tensor(
                        out=r1_sb[:, d, :], in0=r1_sb[:, d, :], in1=t3[:, :], op=ALU.add),
                        inc="ve")
                cb3_hist[cbi] = cb_reads
            h["scum_free"] = p.c("ac")
            h["ct_free"] = p.c("ve")
            lnr = emit_layernorm(
                p, [r1_sb[:, k, :] for k in range(ND)],
                [lnb3v[:, k, :] for k in range(ND)], ps, rows,
                sq_waits=[("ve", ve_corr)],
                bc_waits=[("ve", h.get("dve_xln_prev", 0))],
                xln_waits=[("pe", h.get("pe_mm_prev", 0))])
            h["dve_xln_prev"] = lnr["dve_xln"]

            ac_epi = h.get("ac_epi", 0)
            dve_h_mult = h.get("dve_h_mult", 0)
            pe_mm = 0
            for o in range(NO0):
                wp = o % 2
                p.E("gpsimd", lambda eng, o=o, wp=wp: eng.dma_start(
                    out=wgl_s[wp][:, :], in_=wg_l[o, :, :]),
                    waits=[("pe", gu_used[wp])], inc="dw", amt=16)
                dwv = p.E("gpsimd", lambda eng, o=o, wp=wp: eng.dma_start(
                    out=wul_s[wp][:, :], in_=wu_l[o, :, :]),
                    waits=[("pe", gu_used[wp])], inc="dw", amt=16)

                def gu_fn(eng, wp=wp):
                    for k in range(ND):
                        eng.matmul(ps_p[0][:, :], wgl_s[wp][:, k * 128:(k + 1) * 128],
                                   lnb3v[:, k, :], start=(k == 0), stop=(k == ND - 1))
                    last = None
                    for k in range(ND):
                        last = eng.matmul(ps_p[1][:, :], wul_s[wp][:, k * 128:(k + 1) * 128],
                                          lnb3v[:, k, :], start=(k == 0), stop=(k == ND - 1))
                    return last
                pe_mm = p.E("tensor", gu_fn,
                            waits=[("ve", lnr["dve_xln"]), ("dw", dwv), ("ac", ac_epi),
                                   ("ve", dve_h_mult),
                                   ("ve", cb_reads if o == 0 else 0)],
                            inc="pe")
                gu_used[wp] = pe_mm
                ac_epi = p.E("scalar", lambda eng: eng.activation(
                    out=sg[:, :], in_=ps_p[0][:, :], func=AF.Silu, scale=QSCALE),
                    waits=[("pe", pe_mm), ("ve", dve_h_mult)], inc="ac")
                dve_h_mult = p.E("vector", lambda eng, o=o: eng.scalar_tensor_tensor(
                    out=hb[:, o, :], in0=ps_p[1][:, :], scalar=QSCALE, in1=sg[:, :],
                    op0=ALU.mult, op1=ALU.mult),
                    waits=[("ac", ac_epi)], inc="ve")
            h["ac_epi"] = ac_epi
            h["dve_h_mult"] = dve_h_mult
            h["pe_mm_prev"] = pe_mm
            prev_pe3 = pe_mm

            dve_r2 = h.get("dve_r2", 0)
            for j in range(ND):
                wp = j % 2
                wdl = p.E("gpsimd", lambda eng, j=j, wp=wp: eng.dma_start(
                    out=wd_sb[wp][:, :], in_=wd_l[j, :, :]),
                    waits=[("pe", wd_used[wp])], inc="dw", amt=16)

                def dn_fn(eng, wp=wp):
                    last = None
                    for k in range(NO0):
                        last = eng.matmul(ps_p[2][:, :], wd_sb[wp][:, k * 128:(k + 1) * 128],
                                          hb[:, k, :], start=(k == 0), stop=(k == NO0 - 1))
                    return last
                pe_dn = p.E("tensor", dn_fn,
                            waits=[("dw", wdl), ("ve", dve_h_mult), ("ve", dve_r2)],
                            inc="pe")
                wd_used[wp] = pe_dn
                prev_pe3 = pe_dn
                dve_r2 = p.E("vector", lambda eng, j=j: eng.scalar_tensor_tensor(
                    out=r1_sb[:, j, :], in0=ps_p[2][:, :], scalar=QSCALE,
                    in1=r1_sb[:, j, :], op0=ALU.mult, op1=ALU.add),
                    waits=[("pe", pe_dn)], inc="ve")
            h["dve_r2"] = dve_r2
            h["r1_free"] = dve_r2
            do = p.E("sync", lambda eng, c0=c0: eng.dma_start(
                out=r2_rr[:, :, c0:c0 + TC], in_=r1_sb[:, :, :]),
                waits=[("ve", dve_r2)], inc="do", amt=16)
            if c == NCH - 1:
                do = p.E("sync", lambda eng: eng.dma_start(
                    out=halo_src.ap().rearrange("(k p) t -> p k t", p=128),
                    in_=r1_sb[:, :, :]),
                    waits=[("ve", dve_r2)], inc="do", amt=16)
            h["do"] = do

        cc2 = p.E("gpsimd", lambda eng: eng.collective_compute(
            "AllGather", ALU.bypass, replica_groups=PAIRS,
            ins=[halo_src.ap().opt()], outs=[halo_g.ap().opt()]),
            waits=[("do", p.c("do"))], inc="cc", amt=1)

    p.barrier(extra_sync_waits=[("do", p.c("do")), ("dw", p.c("dw")), ("cc", p.c("cc"))])
    with contextlib.ExitStack() as st:
        w_sb = {n: st.enter_context(nc.sbuf_tensor(f"w_{n}", [128, ND, D], BF16)) for n in "qkvo"}
        maskm_sb = st.enter_context(nc.sbuf_tensor("maskm", [128, 4, 2 * WIN], F32))
        mask0_sb = st.enter_context(nc.sbuf_tensor("mask0", [128, 4, 2 * WIN], F32))
        ident = st.enter_context(nc.sbuf_tensor("ident_sb", [128, 128], BF16))
        r2_sb = st.enter_context(nc.sbuf_tensor("r2_sb", [128, ND, TC], F32))
        lnb4v = st.enter_context(nc.sbuf_tensor("lnb4v4", [128, ND, TC], BF16))
        q_sb = st.enter_context(nc.sbuf_tensor("q_sb", [128, ND, TC], BF16))
        k_sb = [st.enter_context(nc.sbuf_tensor(f"k_sb{i}", [128, ND, TC], BF16)) for i in range(2)]
        v_sb = st.enter_context(nc.sbuf_tensor("v_sb", [128, ND, TC], BF16))
        vt_sb = [st.enter_context(nc.sbuf_tensor(f"vt_sb{i}", [128, 4, NH, DH], BF16)) for i in range(2)]
        s_sb = st.enter_context(nc.sbuf_tensor("s_sb", [128, 2 * WIN], F32))
        p_sb = st.enter_context(nc.sbuf_tensor("p_sb", [128, 2 * WIN], BF16))
        pt_sb = st.enter_context(nc.sbuf_tensor("pt_sb", [128, 8, 128], BF16))
        att_sb = st.enter_context(nc.sbuf_tensor("att_sb", [128, ND, TC], BF16))
        mx = st.enter_context(nc.sbuf_tensor("mx", [128, 1], F32))
        rsum = st.enter_context(nc.sbuf_tensor("rsum", [128, 1], F32))
        ps_pj = [st.enter_context(nc.psum_tensor(f"ps_pj{i}", [128, TC], F32)) for i in range(2)]
        ps_tp = st.enter_context(nc.psum_tensor("ps_tp", [128, 1024], BF16))
        ps_s = st.enter_context(nc.psum_tensor("ps_s", [128, WIN], F32))

        w_rs = {n: ap.rearrange("(k p) o -> p k o", p=128)
                for n, ap in (("q", wqT), ("k", wkT), ("v", wvT), ("o", woT))}
        dw4 = 0
        for n in "qkvo":
            dw4 = p.E("gpsimd", lambda eng, n=n: eng.dma_start(
                out=w_sb[n][:, :, :], in_=w_rs[n]), inc="dw", amt=16)
        dw4 = p.E("gpsimd", lambda eng: eng.dma_start(out=ident[:, :], in_=ident_d),
                  inc="dw", amt=16)
        dc4 = 0
        for (dst, src, three) in ((maskm_sb, mask_m.rearrange("q p k -> p q k"), True),
                                  (mask0_sb, mask_0.rearrange("q p k -> p q k"), True)):
            dc4 = p.E("sync", lambda eng, dst=dst, src=src: eng.dma_start(
                out=dst[:, :, :], in_=src), inc="dx", amt=16)

        h = {}

        def emit_chunk(c):
            par = c % 2
            c0 = c * TC
            src = halo_rr if c < 0 else r2_rr[:, :, c0:c0 + TC]
            dl = p.E("sync", lambda eng, src=src: eng.dma_start(out=r2_sb[:, :, :], in_=src),
                     waits=[("ve", h.get("r2_consumed", 0)), ("do", h.get("do", 0))],
                     inc="dx", amt=16)
            lnr = emit_layernorm(
                p, [r2_sb[:, k, :] for k in range(ND)],
                [lnb4v[:, k, :] for k in range(ND)], ps, rows,
                sq_waits=[("dx", dl)],
                bc_waits=[("ve", h.get("dve_xln_prev", 0))],
                xln_waits=[("pe", h.get("pe_qkv_prev", 0))])
            h["dve_xln_prev"] = lnr["dve_xln"]

            pe_qkv = 0
            ac_ev = h.get("ac_ev", 0)
            names = ("k", "v") if c < 0 else ("q", "k", "v")
            dst_map = {"q": q_sb, "k": k_sb[par], "v": v_sb}
            for wn in names:
                for o in range(ND):
                    pj = o % 2
                    pe_qkv = p.E("tensor", lambda eng, wn=wn, o=o, pj=pj: [
                        eng.matmul(ps_pj[pj][:, :], w_sb[wn][:, k, o * 128:(o + 1) * 128],
                                   lnb4v[:, k, :], start=(k == 0), stop=(k == ND - 1))
                        for k in range(ND)][-1],
                        waits=[("ve", lnr["dve_xln"]), ("dw", dw4),
                               ("ac", h.get(f"ev{pj}", 0)),
                               ("ve", h.get(f"wo{pj}", 0))],
                        inc="pe")
                    ac_ev = p.E("scalar", lambda eng, wn=wn, o=o, pj=pj, dst=dst_map[wn]: eng.copy(
                        out=dst[:, o, :], in_=ps_pj[pj][:, :]),
                        waits=[("pe", pe_qkv), ("pe", h.get("kv_consumed_pe", 0))], inc="ac")
                    h[f"ev{pj}"] = ac_ev
            h["pe_qkv_prev"] = pe_qkv

            ve_tp = h.get("ve_tp", 0)
            for hh in range(NH):
                po = (hh % 2) * 64
                ti = hh // 2
                for s in range(4):
                    pe_vt = p.E("tensor", lambda eng, po=po, ti=ti, s=s, hh=hh: eng.transpose(
                        ps_tp[:, hh * 64:(hh + 1) * 64],
                        v_sb[po:po + 64, ti, s * 128:(s + 1) * 128], ident[po:po + 64, po:po + 64]),
                        waits=[("ac", ac_ev), ("ve", ve_tp),
                               ("pe", h.get("vt_consumed_pe", 0))], inc="pe")
                    ve_tp = p.E("vector", lambda eng, par=par, s=s, hh=hh: eng.tensor_copy(
                        out=vt_sb[par][:, s, hh, :], in_=ps_tp[:, hh * 64:(hh + 1) * 64]),
                        waits=[("pe", pe_vt)], inc="ve")
            h["ve_tp"] = ve_tp
            ve_vt_done = ve_tp
            if c < 0:
                h["r2_consumed"] = lnr["dve_xln"]
                return

            ve_att = 0
            for hh in range(NH):
                po = (hh % 2) * 64
                ti = hh // 2
                for qc in range(4):
                    q_ap = q_sb[po:po + 64, ti, qc * 128:(qc + 1) * 128]
                    mrow = (mask0_sb if c == 0 else maskm_sb)
                    for half in range(2):
                        kb = k_sb[1 - par] if half == 0 else k_sb[par]
                        pe_s = p.E("tensor", lambda eng, q_ap=q_ap, kb=kb, po=po, ti=ti: eng.matmul(
                            ps_s[:, :], q_ap, kb[po:po + 64, ti, :], start=True, stop=True),
                            waits=[("ve", h.get("ve_s", 0)), ("ac", ac_ev)],
                            inc="pe")
                        ve_s = p.E("vector", lambda eng, qc=qc, half=half, mrow=mrow: eng.scalar_tensor_tensor(
                            out=s_sb[:, half * WIN:(half + 1) * WIN], in0=ps_s[:, :],
                            scalar=0.125, in1=mrow[:, qc, half * WIN:(half + 1) * WIN],
                            op0=ALU.mult, op1=ALU.add),
                            waits=[("pe", pe_s), ("dx", dc4)], inc="ve")
                        h["ve_s"] = ve_s
                    ve_mx = p.E("vector", lambda eng: eng.tensor_reduce(
                        out=mx[:, :], in_=s_sb[:, :], axis=AX.X, op=ALU.max, negate=True),
                        waits=[("ve", h["ve_s"])], inc="ve")
                    ac_p = p.E("scalar", lambda eng: eng.activation(
                        out=p_sb[:, :], in_=s_sb[:, :], func=AF.Exp, bias=mx[:, :]),
                        waits=[("ve", ve_mx)], inc="ac")
                    ve_rs = p.E("vector", lambda eng: eng.tensor_reduce(
                        out=rsum[:, :], in_=p_sb[:, :], axis=AX.X, op=ALU.add),
                        waits=[("ac", ac_p)], inc="ve")
                    ve_rc = p.E("vector", lambda eng: eng.reciprocal(
                        out=rsum[:, :], in_=rsum[:, :]),
                        waits=[("ve", ve_rs)], inc="ve")
                    ve_sm = p.E("vector", lambda eng: eng.tensor_scalar_mul(
                        out=p_sb[:, :], in0=p_sb[:, :], scalar1=rsum[:, :]),
                        waits=[("ve", ve_rc)], inc="ve")
                    pe_pt = p.E("tensor", lambda eng: [
                        eng.transpose(ps_tp[:, s * 128:(s + 1) * 128],
                                      p_sb[:, s * 128:(s + 1) * 128], ident[:, :])
                        for s in range(8)][-1],
                        waits=[("ve", ve_sm), ("ve", h.get("ve_tp", 0))], inc="pe")
                    ve_ptev = p.E("vector", lambda eng: eng.tensor_copy(
                        out=pt_sb[:, :, :], in_=ps_tp[:, :]),
                        waits=[("pe", pe_pt)], inc="ve")
                    h["ve_tp"] = ve_ptev

                    def pv_fn(eng, par=par, hh=hh):
                        last = None
                        for s in range(8):
                            vts = vt_sb[1 - par] if s < 4 else vt_sb[par]
                            last = eng.matmul(ps_s[0:64, 0:128], vts[:, s % 4, hh, :],
                                              pt_sb[:, s, :], start=(s == 0), stop=(s == 7))
                        return last
                    pe_pv = p.E("tensor", pv_fn,
                                waits=[("ve", ve_ptev), ("ve", h["ve_s"]),
                                       ("ve", ve_vt_done)], inc="pe")
                    h["vt_consumed_pe"] = pe_pv
                    ve_att = p.E("vector", lambda eng, po=po, ti=ti, qc=qc: eng.tensor_copy(
                        out=att_sb[po:po + 64, ti, qc * 128:(qc + 1) * 128], in_=ps_s[0:64, 0:128]),
                        waits=[("pe", pe_pv)], inc="ve")
                    h["ve_s"] = ve_att

            dve_r3 = h.get("dve_r3", 0)
            for o in range(ND):
                pj = o % 2
                pe_wo = p.E("tensor", lambda eng, o=o, pj=pj: [
                    eng.matmul(ps_pj[pj][:, :], w_sb["o"][:, k, o * 128:(o + 1) * 128],
                               att_sb[:, k, :], start=(k == 0), stop=(k == ND - 1))
                    for k in range(ND)][-1],
                    waits=[("ve", ve_att), ("ve", h.get(f"wo{pj}", 0))], inc="pe")
                h["kv_consumed_pe"] = pe_wo
                dve_r3 = p.E("vector", lambda eng, o=o, pj=pj: eng.tensor_tensor(
                    out=r2_sb[:, o, :], in0=ps_pj[pj][:, :], in1=r2_sb[:, o, :], op=ALU.add),
                    waits=[("pe", pe_wo)], inc="ve")
                h[f"wo{pj}"] = dve_r3
            h["dve_r3"] = dve_r3
            do = p.E("sync", lambda eng, c0=c0: eng.dma_start(
                out=r3_rr[:, :, c0:c0 + TC], in_=r2_sb[:, :, :]),
                waits=[("ve", dve_r3)], inc="do", amt=16)
            h["do"] = do
            h["r2_consumed"] = dve_r3

        emit_chunk(-1)
        for c in range(NCH):
            emit_chunk(c)

    p.barrier(extra_sync_waits=[("do", p.c("do")), ("dw", p.c("dw"))])
    with contextlib.ExitStack() as st:
        sc_sb = {n: st.enter_context(nc.sbuf_tensor(
            "sc_" + n, [128, NO1 if n != "s3c" else ND], F32)) for n in scd}
        r3_sb = st.enter_context(nc.sbuf_tensor("r3_sb", [128, ND, TCE], F32))
        lnb5v = st.enter_context(nc.sbuf_tensor("lnb5v5", [128, ND, TCE], BF16))
        hbuf = st.enter_context(nc.sbuf_tensor("hbuf", [128, NO1, TCE], BF16))
        gbuf = st.enter_context(nc.sbuf_tensor("gbuf", [128, NO1, TCE], BF16))
        _wstr0 = st.enter_context(nc.sbuf_tensor("wstr0", [128, H1], BF16))
        wstr = [_wstr0, _wstr0]
        t1f = st.enter_context(nc.sbuf_tensor("t1f", [128, TCE], BF16))
        ps_a = st.enter_context(nc.psum_tensor("ps_a", [128, TCE], F32))
        ps_b = st.enter_context(nc.psum_tensor("ps_b", [128, TCE], F32))

        dc5 = 0
        for n in scd:
            dc5 = p.E("sync", lambda eng, n=n: eng.dma_start(out=sc_sb[n][:, :], in_=scd[n]),
                      inc="dx", amt=16)

        h = {"w12_used": [0, 0], "wstr_used": [0, 0]}
        for c in range(NCHE):
            c0 = c * TCE
            dl = p.E("sync", lambda eng, c0=c0: eng.dma_start(
                out=r3_sb[:, :, :], in_=r3_rr[:, :, c0:c0 + TCE]),
                waits=[("do", h.get("do", 0)), ("ve", h.get("r3_free", 0))], inc="dx", amt=16)
            xln_done = 0
            for sub in range(2):
                lnr = emit_layernorm(
                    p, [r3_sb[:, k, sub * TC:(sub + 1) * TC] for k in range(ND)],
                    [lnb5v[:, k, sub * TC:(sub + 1) * TC] for k in range(ND)], ps, rows,
                    sq_waits=[("dx", dl)],
                    bc_waits=[("ve", h.get("dve_xln_prev", 0))],
                    xln_waits=[("pe", h.get("pe_mm_prev", 0))])
                h["dve_xln_prev"] = lnr["dve_xln"]
                xln_done = lnr["dve_xln"]

            ac_epi = h.get("ac_epi", 0)
            ve_hb = h.get("ve_hb", 0)
            w12_used = h["w12_used"]
            wstr_used = h["wstr_used"]
            pe_mm = 0
            for o in range(NO1):
                wp = o % 2
                p.E("gpsimd", lambda eng, o=o, wp=wp: eng.dma_start(
                    out=wstr[wp][:, D:2 * D], in_=w1_l[o, :, :]),
                    waits=[("pe", w12_used[wp])], inc="dwE", amt=16)
                dwv = p.E("gpsimd", lambda eng, o=o, wp=wp: eng.dma_start(
                    out=wstr[wp][:, 0:D], in_=w2_l[o, :, :]),
                    waits=[("pe", wstr_used[wp])], inc="dwE", amt=16)

                def mm12(eng, o=o, wp=wp):
                    for k in range(ND):
                        for sub in range(2):
                            eng.matmul(ps_a[:, sub * TC:(sub + 1) * TC],
                                       wstr[wp][:, D + k * 128:D + (k + 1) * 128],
                                       lnb5v[:, k, sub * TC:(sub + 1) * TC],
                                       start=(k == 0), stop=(k == ND - 1))
                    last = None
                    for k in range(ND):
                        for sub in range(2):
                            last = eng.matmul(ps_b[:, sub * TC:(sub + 1) * TC],
                                              wstr[wp][:, k * 128:(k + 1) * 128],
                                              lnb5v[:, k, sub * TC:(sub + 1) * TC],
                                              start=(k == 0), stop=(k == ND - 1))
                    return last
                pe_mm = p.E("tensor", mm12,
                            waits=[("dwE", dwv), ("ve", xln_done), ("ac", ac_epi),
                                   ("ve", ve_hb)], inc="pe")
                w12_used[0] = w12_used[1] = pe_mm
                wstr_used[0] = wstr_used[1] = pe_mm
                ac_epi = p.E("scalar", lambda eng, o=o: eng.activation(
                    out=t1f[:, :], in_=ps_a[:, :], func=AF.Silu,
                    scale=sc_sb["s1c"][:, o:o + 1]),
                    waits=[("pe", pe_mm), ("dx", dc5), ("ve", ve_hb)], inc="ac")
                ve_hb = p.E("vector", lambda eng, o=o: eng.scalar_tensor_tensor(
                    out=hbuf[:, o, :], in0=ps_b[:, :], scalar=sc_sb["s2c"][:, o:o + 1],
                    in1=t1f[:, :], op0=ALU.mult, op1=ALU.mult),
                    waits=[("ac", ac_epi), ("pe", h.get("w3_last_pe", 0))], inc="ve")
            h["ac_epi"] = ac_epi
            h["ve_hb"] = ve_hb

            def big_layer(w_dram, rhs_buf, dst_buf, scol, func, used_init, ac_init,
                          rhs_ready_ve, rhs_ready_ac, nouts=NO1, nk=NO1):
                ac_last = ac_init
                used = list(used_init)
                for o in range(nouts):
                    wp = o % 2
                    dwv2 = p.E("gpsimd", lambda eng, o=o, wp=wp, w_dram=w_dram: eng.dma_start(
                        out=wstr[wp][:, :], in_=w_dram[o, :, :]),
                        waits=[("pe", used[wp])], inc="dwE", amt=16)

                    def mmf(eng, o=o, wp=wp, rhs_buf=rhs_buf, nk=nk):
                        last = None
                        for k in range(nk):
                            for sub in range(2):
                                last = eng.matmul(ps_a[:, sub * TC:(sub + 1) * TC],
                                                  wstr[wp][:, k * 128:(k + 1) * 128],
                                                  rhs_buf[:, k, sub * TC:(sub + 1) * TC],
                                                  start=(k == 0), stop=(k == nk - 1))
                        return last
                    pe_f = p.E("tensor", mmf,
                               waits=[("dwE", dwv2), ("ve", rhs_ready_ve),
                                      ("ac", max(ac_last, rhs_ready_ac))], inc="pe")
                    used[0] = used[1] = pe_f
                    ac_last = p.E("scalar", lambda eng, o=o, dst_buf=dst_buf, scol=scol, func=func: eng.activation(
                        out=dst_buf[:, o, :], in_=ps_a[:, :], func=func,
                        scale=sc_sb[scol][:, o:o + 1]),
                        waits=[("pe", pe_f)], inc="ac")
                return used, ac_last

            used, ac_g1 = big_layer(h1_l, hbuf, gbuf, "sh1c", AF.Gelu,
                                    [w12_used[0], w12_used[1]], ac_epi, ve_hb, 0)
            used, ac_g2 = big_layer(h2_l, gbuf, hbuf, "sh2c", AF.Gelu,
                                    used, ac_g1, 0, ac_g1)

            dve_out = h.get("dve_out", 0)
            used3 = list(used)
            pe_3 = 0
            for j in range(ND):
                wp = j % 2
                dwv3 = p.E("gpsimd", lambda eng, j=j, wp=wp: eng.dma_start(
                    out=wstr[wp][:, :], in_=w3_l[j, :, :]),
                    waits=[("pe", used3[wp])], inc="dwE", amt=16)

                def mm3(eng, j=j, wp=wp):
                    last = None
                    for k in range(NO1):
                        for sub in range(2):
                            last = eng.matmul(ps_a[:, sub * TC:(sub + 1) * TC],
                                              wstr[wp][:, k * 128:(k + 1) * 128],
                                              hbuf[:, k, sub * TC:(sub + 1) * TC],
                                              start=(k == 0), stop=(k == NO1 - 1))
                    return last
                pe_3 = p.E("tensor", mm3,
                           waits=[("dwE", dwv3), ("ac", ac_g2), ("ve", dve_out)], inc="pe")
                used3[0] = used3[1] = pe_3
                dve_out = p.E("vector", lambda eng, j=j: eng.scalar_tensor_tensor(
                    out=r3_sb[:, j, :], in0=ps_a[:, :], scalar=sc_sb["s3c"][:, j:j + 1],
                    in1=r3_sb[:, j, :], op0=ALU.mult, op1=ALU.add),
                    waits=[("pe", pe_3)], inc="ve")
            h["w12_used"] = [pe_3, pe_3]
            h["wstr_used"] = [pe_3, pe_3]
            h["w3_last_pe"] = pe_3
            h["dve_out"] = dve_out
            do = p.E("sync", lambda eng, c0=c0: eng.dma_start(
                out=out_rr[:, :, c0:c0 + TCE], in_=r3_sb[:, :, :]),
                waits=[("ve", dve_out)], inc="do", amt=16)
            h["do"] = do
            h["r3_free"] = dve_out
            h["pe_mm_prev"] = p.c("pe")

    p.E("sync", None, waits=[("do", p.c("do"))])
    p.emit()
    return nc


def _lhsT_stream(W, o_blk=128):
    W = np.ascontiguousarray(W)
    O, I = W.shape
    nj = O // o_blk
    return np.ascontiguousarray(
        W.reshape(nj, o_blk, I // 128, 128).transpose(0, 3, 2, 1)).reshape(nj, 128, (I // 128) * o_blk)


def _build_masks():
    i = np.arange(WIN)[:, None]
    j = np.arange(2 * WIN)[None, :]
    base = (j > i) & (j <= i + WIN)
    m_main = np.where(base, 0.0, -1e30).astype(np.float32)
    m_blk0 = np.where(base & (j >= WIN), 0.0, -1e30).astype(np.float32)
    return (np.ascontiguousarray(m_main.reshape(4, 128, 2 * WIN)),
            np.ascontiguousarray(m_blk0.reshape(4, 128, 2 * WIN)))


_NC_CACHE = {}


def kernel(**inp):
    f32 = np.float32
    cores = list(range(NCORES))
    x_fm = np.ascontiguousarray(np.asarray(inp["L0_attn_x"], f32).transpose(0, 2, 1))
    ones_col = np.ones((128, 1), f32)
    ones_row = np.ones((1, 128), f32)
    m_main, m_blk0 = _build_masks()
    logA = np.asarray(inp["L0_attn_log_A"], f32)

    shared = {
        "wbT": np.ascontiguousarray(np.asarray(inp["L0_attn_W_B"], f32).T),
        "wcT": np.ascontiguousarray(np.asarray(inp["L0_attn_W_C"], f32).T),
        "wdt1T": np.ascontiguousarray(np.asarray(inp["L0_attn_W_dt1"], f32).T),
        "wdt2T": np.ascontiguousarray(np.asarray(inp["L0_attn_W_dt2"], f32).T).reshape(DTR, ND, 128),
        "bdt2c": np.ascontiguousarray(np.asarray(inp["L0_attn_b_dt2"], f32).reshape(ND, 128).T),
        "logA_pn": np.ascontiguousarray(
            logA.reshape(ND, 128, NS).transpose(1, 0, 2)).reshape(128, ND * NS),
        "logAT": np.ascontiguousarray(logA.T),
        "eye16": np.eye(NS, dtype=f32),
        "dskc": np.ascontiguousarray(np.asarray(inp["L0_attn_D_skip"], f32).reshape(ND, 128).T),
        "wg_l": _lhsT_stream(np.asarray(inp["L0_ffn_w_gate"])),
        "wu_l": _lhsT_stream(np.asarray(inp["L0_ffn_w_up"])),
        "wd_l": _lhsT_stream(np.asarray(inp["L0_ffn_w_down"])),
        "wqT": np.ascontiguousarray(np.asarray(inp["L1_attn_Wq"], f32).T),
        "wkT": np.ascontiguousarray(np.asarray(inp["L1_attn_Wk"], f32).T),
        "wvT": np.ascontiguousarray(np.asarray(inp["L1_attn_Wv"], f32).T),
        "woT": np.ascontiguousarray(np.asarray(inp["L1_attn_Wo"], f32).T),
        "mask_m": m_main,
        "ident": np.eye(128).astype(ml_dtypes.bfloat16),
        "w1_l": _lhsT_stream(np.asarray(inp["L1_ffn_w1_w"], f32)),
        "w2_l": _lhsT_stream(np.asarray(inp["L1_ffn_w2_w"], f32)),
        "h1_l": _lhsT_stream(np.asarray(inp["L1_ffn_h1_w"], f32)),
        "h2_l": _lhsT_stream(np.asarray(inp["L1_ffn_h2_w"], f32)),
        "w3_l": _lhsT_stream(np.asarray(inp["L1_ffn_w3_w"], f32)),
        "s1c": np.ascontiguousarray(np.asarray(inp["L1_ffn_w1_s"], f32).reshape(NO1, 128).T),
        "s2c": np.ascontiguousarray(np.asarray(inp["L1_ffn_w2_s"], f32).reshape(NO1, 128).T),
        "sh1c": np.ascontiguousarray(np.asarray(inp["L1_ffn_h1_s"], f32).reshape(NO1, 128).T),
        "sh2c": np.ascontiguousarray(np.asarray(inp["L1_ffn_h2_s"], f32).reshape(NO1, 128).T),
        "s3c": np.ascontiguousarray(np.asarray(inp["L1_ffn_w3_s"], f32).reshape(ND, 128).T),
        "ones_col": ones_col,
        "ones_row": ones_row,
    }

    ims = []
    for c in cores:
        b, hf = c // 2, c % 2
        im = dict(shared)
        im["xT"] = np.ascontiguousarray(x_fm[b, :, hf * TPC:(hf + 1) * TPC])
        im["flag"] = np.full((128, 1), float(hf), f32)
        im["mask_0"] = m_blk0 if hf == 0 else m_main
        ims.append(im)

    nc = build_fused()
    res = run_bass_kernel_spmd(nc, ims, core_ids=cores)
    out = np.empty((B, T, D), f32)
    for c in cores:
        b, hf = c // 2, c % 2
        out[b, hf * TPC:(hf + 1) * TPC, :] = res.results[c]["outT"].T
    return out


# version 03 (final)
# speedup vs baseline: 1.1706x; 1.1706x over previous
import contextlib
import numpy as np
import ml_dtypes
import concourse.bass as bass
import concourse.mybir as mybir
from concourse.bass_utils import run_bass_kernel_spmd

F32 = mybir.dt.float32
BF16 = mybir.dt.bfloat16
I32 = mybir.dt.int32
AF = mybir.ActivationFunctionType
ALU = mybir.AluOpType
AX = mybir.AxisListType

NCORES = 8
B, T, D = 4, 8192, 1024
TPC = 4096
TC = 512
NCH = TPC // TC
ND = D // 128
NS = 16
DTR = 64
NH, DH, WIN = 16, 64, 512
H0 = 3072
H1 = 4096
EPS = 1e-5
QSCALE = 1.0 / 127.0
NO0 = H0 // 128
NO1 = H1 // 128
TCE = 1024
NCHE = TPC // TCE
ENGS = ("sync", "tensor", "vector", "scalar", "gpsimd")
PAIRS = [[0, 1], [2, 3], [4, 5], [6, 7]]


class Prog:
    def __init__(self, nc):
        self.nc = nc
        self.items = []
        self.cnt = {}
        self.pending_bar = set()
        self.bar_target = 0
        self.phase = "pre"

    def E(self, eng, fn, waits=(), inc=None, amt=1):
        w = {}
        if eng in self.pending_bar:
            w["ph"] = self.bar_target
            self.pending_bar.discard(eng)
        for (s, v) in waits:
            if v > 0:
                w[s] = max(w.get(s, 0), v)
        if inc is not None:
            self.cnt[inc] = self.cnt.get(inc, 0) + amt
        self.items.append((eng, fn, sorted(w.items()), inc, amt, self.phase))
        return self.cnt.get(inc, 0) if inc else 0

    def c(self, name):
        return self.cnt.get(name, 0)

    def barrier(self, extra_sync_waits=()):
        self.E("sync", lambda eng: eng.nop(), waits=list(extra_sync_waits), inc="ph")
        for e in ("tensor", "vector", "scalar", "gpsimd"):
            self.E(e, lambda eng: eng.nop(), inc="ph")
        self.bar_target = self.c("ph")
        self.pending_bar = set(ENGS)

    def emit(self):
        nc = self.nc
        names = sorted({n for it in self.items for (n, _) in it[2]}
                       | {it[3] for it in self.items if it[3]})
        sems = {n: nc.alloc_semaphore(name="s_" + n) for n in names}
        with nc.Block() as block:
            for eng_name in ENGS:
                def body(eng, _en=eng_name):
                    for (e, fn, waits, inc, amt, phase) in self.items:
                        if e != _en:
                            continue
                        for (s, v) in waits:
                            eng.wait_ge(sems[s], v)
                        with nc.named_scope(phase):
                            ins = fn(eng) if fn is not None else None
                            if inc is not None:
                                ins.then_inc(sems[inc], amt)
                getattr(block, eng_name)(body)


def emit_layernorm(p, src_tiles, xlnb, ps, rows, sq_waits, bc_waits, xln_waits,
                   xln32_tiles=None, tc=TC):
    nd = len(src_tiles)
    onec, oner = rows["onec"], rows["oner"]
    x2 = rows["x2"]

    def sx_fn(eng):
        last = None
        for k, ap in enumerate(src_tiles):
            last = eng.matmul(ps["sx"][:, :], onec[:, :], ap, start=(k == 0), stop=(k == nd - 1))
        return last
    pe_st = p.E("tensor", sx_fn,
                waits=list(sq_waits) + [("ve", getattr(p, "ln_r1_prev", 0))], inc="pe")
    if not hasattr(p, "ln_x2_free"):
        p.ln_x2_free = 0
    for k, ap in enumerate(src_tiles):
        sq_done = p.E("scalar", lambda eng, ap=ap: eng.activation(
            out=x2[:, 0:tc], in_=ap, func=AF.Square),
            waits=list(sq_waits) + [("pe", p.ln_x2_free)], inc="ac")
        pe_st = p.E("tensor", lambda eng, k=k: eng.matmul(
            ps["sx2"][:, :], onec[:, :], x2[:, 0:tc],
            start=(k == 0), stop=(k == nd - 1)),
            waits=[("ac", sq_done)], inc="pe")
        p.ln_x2_free = pe_st

    def r1_fn(eng):
        eng.tensor_scalar_mul(out=rows["mean"][:, :], in0=ps["sx"][:, :], scalar1=1.0 / D)
        eng.tensor_scalar_mul(out=rows["ta"][:, :], in0=ps["sx2"][:, :], scalar1=1.0 / D)
        eng.tensor_mul(out=rows["tb"][:, :], in0=rows["mean"][:, :], in1=rows["mean"][:, :])
        eng.tensor_sub(out=rows["ta"][:, :], in0=rows["ta"][:, :], in1=rows["tb"][:, :])
        return eng.tensor_scalar_add(out=rows["ta"][:, :], in0=rows["ta"][:, :], scalar1=EPS)
    dve_r1 = p.E("vector", r1_fn, waits=[("pe", pe_st)], inc="ve")

    a_sqr = p.E("scalar", lambda eng: eng.activation(
        out=rows["tb"][:, :], in_=rows["ta"][:, :], func=AF.Sqrt),
        waits=[("ve", dve_r1)], inc="ac")

    def r2_fn(eng):
        eng.reciprocal(out=rows["rstd"][:, :], in_=rows["tb"][:, :])
        return eng.tensor_mul(out=rows["mr"][:, :], in0=rows["mean"][:, :], in1=rows["rstd"][:, :])
    dve_r2 = p.E("vector", r2_fn, waits=[("ac", a_sqr)], inc="ve")
    p.ln_r1_prev = dve_r1

    def bc_fn(eng):
        eng.matmul(ps["b1"][:, :], oner[:, :], rows["rstd"][:, :], start=True, stop=True)
        return eng.matmul(ps["b2"][:, :], oner[:, :], rows["mr"][:, :], start=True, stop=True)
    pe_bc = p.E("tensor", bc_fn, waits=[("ve", dve_r2)] + list(bc_waits), inc="pe")

    tmp0 = rows["tmp0"]

    def xln_fn(eng):
        last = None
        for k, ap in enumerate(src_tiles):
            eng.tensor_tensor(out=tmp0[:, :], in0=ap, in1=ps["b1"][:, :], op=ALU.mult)
            dst32 = xln32_tiles[k] if xln32_tiles is not None else tmp0[:, :]
            eng.tensor_tensor(out=dst32, in0=tmp0[:, :], in1=ps["b2"][:, :], op=ALU.subtract)
            last = eng.tensor_copy(out=xlnb[k], in_=dst32)
        return last
    dve_x = p.E("vector", xln_fn, waits=[("pe", pe_bc)] + list(xln_waits), inc="ve")
    return {"dve_xln": dve_x, "pe_bc": pe_bc, "pe_st": pe_st}


def build_fused():
    nc = bass.Bass("TRN2", target_bir_lowering=False, debug=False)
    p = Prog(nc)

    xT = nc.dram_tensor("xT", [D, TPC], F32, kind="ExternalInput").ap()
    wbT = nc.dram_tensor("wbT", [D, NS], F32, kind="ExternalInput").ap()
    wcT = nc.dram_tensor("wcT", [D, NS], F32, kind="ExternalInput").ap()
    wdt1T = nc.dram_tensor("wdt1T", [D, DTR], F32, kind="ExternalInput").ap()
    wdt2T = nc.dram_tensor("wdt2T", [DTR, ND, 128], F32, kind="ExternalInput").ap()
    bdt2c = nc.dram_tensor("bdt2c", [128, ND], F32, kind="ExternalInput").ap()
    logA_pn = nc.dram_tensor("logA_pn", [128, ND * NS], F32, kind="ExternalInput").ap()
    logAT = nc.dram_tensor("logAT", [NS, D], F32, kind="ExternalInput").ap()
    eye16_d = nc.dram_tensor("eye16", [NS, NS], F32, kind="ExternalInput").ap()
    dskc = nc.dram_tensor("dskc", [128, ND], F32, kind="ExternalInput").ap()
    flag_d = nc.dram_tensor("flag", [128, 1], F32, kind="ExternalInput").ap()
    wg_l = nc.dram_tensor("wg_l", [NO0, 128, D], I32, kind="ExternalInput").ap()
    wu_l = nc.dram_tensor("wu_l", [NO0, 128, D], I32, kind="ExternalInput").ap()
    wd_l = nc.dram_tensor("wd_l", [ND, 128, H0], I32, kind="ExternalInput").ap()
    wqT = nc.dram_tensor("wqT", [D, D], F32, kind="ExternalInput").ap()
    wkT = nc.dram_tensor("wkT", [D, D], F32, kind="ExternalInput").ap()
    wvT = nc.dram_tensor("wvT", [D, D], F32, kind="ExternalInput").ap()
    woT = nc.dram_tensor("woT", [D, D], F32, kind="ExternalInput").ap()
    mask_m = nc.dram_tensor("mask_m", [4, 128, 2 * WIN], F32, kind="ExternalInput").ap()
    mask_0 = nc.dram_tensor("mask_0", [4, 128, 2 * WIN], F32, kind="ExternalInput").ap()
    ident_d = nc.dram_tensor("ident", [128, 128], BF16, kind="ExternalInput").ap()
    w1_l = nc.dram_tensor("w1_l", [NO1, 128, D], F32, kind="ExternalInput").ap()
    w2_l = nc.dram_tensor("w2_l", [NO1, 128, D], F32, kind="ExternalInput").ap()
    h1_l = nc.dram_tensor("h1_l", [NO1, 128, H1], F32, kind="ExternalInput").ap()
    h2_l = nc.dram_tensor("h2_l", [NO1, 128, H1], F32, kind="ExternalInput").ap()
    w3_l = nc.dram_tensor("w3_l", [ND, 128, H1], F32, kind="ExternalInput").ap()
    scd = {n: nc.dram_tensor(n, [128, NO1 if n != "s3c" else ND], F32, kind="ExternalInput").ap()
           for n in ("s1c", "s2c", "sh1c", "sh2c", "s3c")}
    onec_d = nc.dram_tensor("ones_col", [128, 1], F32, kind="ExternalInput").ap()
    oner_d = nc.dram_tensor("ones_row", [1, 128], F32, kind="ExternalInput").ap()
    out_o = nc.dram_tensor("outT", [D, TPC], F32, kind="ExternalOutput").ap()

    r1d = nc.dram_tensor("r1d", [D, TPC], F32)
    scumd = nc.dram_tensor("scumd", [D, TPC], F32)
    ctd = nc.dram_tensor("ctd", [NS, TPC], F32)
    r2d = nc.dram_tensor("r2d", [D, TPC], F32)
    r3d = nc.dram_tensor("r3d", [D, TPC], F32)
    hfin_src = nc.dram_tensor("hfin_src", [128, ND * NS], F32)
    hfin_g = nc.dram_tensor("hfin_g", [256, ND * NS], F32)
    halo_src = nc.dram_tensor("halo_src", [D, WIN], F32)
    halo_g = nc.dram_tensor("halo_g", [2 * D, WIN], F32)

    xT_r = xT.rearrange("(k p) t -> p k t", p=128)
    r1_rr = r1d.ap().rearrange("(k p) t -> p k t", p=128)
    scum_rr = scumd.ap().rearrange("(k p) t -> p k t", p=128)
    r2_rr = r2d.ap().rearrange("(k p) t -> p k t", p=128)
    r3_rr = r3d.ap().rearrange("(k p) t -> p k t", p=128)
    halo_rr = halo_g.ap()[0:D, :].rearrange("(k p) t -> p k t", p=128)
    out_rr = out_o.rearrange("(k p) t -> p k t", p=128)

    rows = {n: nc.alloc_sbuf_tensor("r_" + n, [1, TC], F32)
            for n in ("mean", "ta", "tb", "rstd", "mr")}
    rows["onec"] = nc.alloc_sbuf_tensor("onec", [128, 1], F32)
    rows["oner"] = nc.alloc_sbuf_tensor("oner", [1, 128], F32)
    rows["x2"] = nc.alloc_sbuf_tensor("x2", [128, TC], F32)
    rows["tmp0"] = nc.alloc_sbuf_tensor("tmp0", [128, TC], F32)
    ps = {
        "sx": nc.alloc_psum_tensor("ps_sx", [1, TC], F32),
        "sx2": nc.alloc_psum_tensor("ps_sx2", [1, TC], F32),
        "b1": nc.alloc_psum_tensor("ps_b1", [128, TC], F32),
        "b2": nc.alloc_psum_tensor("ps_b2", [128, TC], F32),
    }
    A_pn = nc.alloc_sbuf_tensor("A_pn", [128, ND * NS], F32)
    SA = nc.alloc_sbuf_tensor("SA", [NS, D], F32)
    ones16 = nc.alloc_sbuf_tensor("ones16", [NS, 128], F32)
    eye16 = nc.alloc_sbuf_tensor("eye16s", [NS, NS], F32)
    dsk = nc.alloc_sbuf_tensor("dsk", [128, ND], F32)
    flag_sb = nc.alloc_sbuf_tensor("flag_sb", [128, 1], F32)
    h_in = nc.alloc_sbuf_tensor("h_in", [128, ND * NS], F32)
    carry = nc.alloc_sbuf_tensor("carry", [128, ND * NS], F32)
    carry_s = nc.alloc_sbuf_tensor("carry_s", [128, ND], F32)

    dc = 0
    for (dst, src) in ((rows["onec"], onec_d), (rows["oner"], oner_d),
                       (A_pn, logA_pn), (SA, logAT), (eye16, eye16_d),
                       (dsk, dskc), (flag_sb, flag_d)):
        dc = p.E("sync", lambda eng, dst=dst, src=src: eng.dma_start(
            out=dst[:, :], in_=src), inc="dx", amt=16)
    apl = p.E("scalar", lambda eng: (
        eng.activation(out=A_pn[:, :], in_=A_pn[:, :], func=AF.Exp),
        eng.activation(out=SA[:, :], in_=SA[:, :], func=AF.Exp))[-1],
        waits=[("dx", dc)], inc="ac")

    def pl_v(eng):
        eng.tensor_scalar_mul(out=A_pn[:, :], in0=A_pn[:, :], scalar1=-1.0)
        eng.tensor_scalar(out=SA[:, :], in0=SA[:, :], scalar1=-1.0, scalar2=1e-8,
                          op0=ALU.mult, op1=ALU.add)
        eng.reciprocal(out=SA[:, :], in_=SA[:, :])
        eng.memset(ones16[:, :], 1.0)
        eng.memset(carry[:, :], 0.0)
        return eng.memset(carry_s[:, :], 0.0)
    vpl = p.E("vector", pl_v, waits=[("ac", apl)], inc="ve")

    p.phase = "ph12"
    with contextlib.ExitStack() as st:
        wbl = st.enter_context(nc.sbuf_tensor("wbl", [128, ND, NS], BF16))
        wcl = st.enter_context(nc.sbuf_tensor("wcl", [128, ND, NS], BF16))
        wdt1l = st.enter_context(nc.sbuf_tensor("wdt1l", [128, ND, DTR], BF16))
        wdt2l = st.enter_context(nc.sbuf_tensor("wdt2l", [DTR, ND, 128], BF16))
        bdt2 = st.enter_context(nc.sbuf_tensor("bdt2", [128, ND], F32))
        xs = [st.enter_context(nc.sbuf_tensor(f"xs{i}", [128, ND, TC], F32)) for i in range(2)]
        xlns = [st.enter_context(nc.sbuf_tensor(f"xlns{i}", [128, ND, TC], F32)) for i in range(2)]
        xlnb = [st.enter_context(nc.sbuf_tensor(f"xlnb{i}", [128, ND, TC], BF16)) for i in range(2)]
        dts = st.enter_context(nc.sbuf_tensor("dts", [128, ND, TC], F32))
        scums = st.enter_context(nc.sbuf_tensor("scums", [128, ND, TC], F32))
        y_sb = [st.enter_context(nc.sbuf_tensor(f"y_sb{i}", [128, ND, TC], F32)) for i in range(2)]
        bt_sb = st.enter_context(nc.sbuf_tensor("bt_sb", [NS, TC], F32))
        ct_sb = st.enter_context(nc.sbuf_tensor("ct_sb", [NS, TC], F32))
        t1b = st.enter_context(nc.sbuf_tensor("t1b", [DTR, TC], BF16))
        l_sp = st.enter_context(nc.sbuf_tensor("l_sp", [128, TC], F32))
        r_sp = st.enter_context(nc.sbuf_tensor("r_sp", [128, TC], F32))
        onesT = st.enter_context(nc.sbuf_tensor("onesT", [128, TC], F32))
        abar = [st.enter_context(nc.sbuf_tensor(f"abar{i}", [128, TC], F32)) for i in range(2)]
        t2 = st.enter_context(nc.sbuf_tensor("t2", [128, TC], F32))
        b_t = st.enter_context(nc.sbuf_tensor("b_t", [128, TC], F32))
        h_t = st.enter_context(nc.sbuf_tensor("h_t", [128, TC], F32))
        tmpy = st.enter_context(nc.sbuf_tensor("tmpy", [128, TC], F32))
        ps_m = [st.enter_context(nc.psum_tensor(f"ps_m{i}", [128, TC], F32)) for i in range(2)]
        ps_rab = [st.enter_context(nc.psum_tensor(f"ps_rab{i}", [128, TC], F32)) for i in range(2)]

        dw = 0
        for (dst, src) in ((wbl, wbT.rearrange("(k p) n -> p k n", p=128)),
                           (wcl, wcT.rearrange("(k p) n -> p k n", p=128)),
                           (wdt1l, wdt1T.rearrange("(k p) n -> p k n", p=128))):
            dw = p.E("gpsimd", lambda eng, dst=dst, src=src: eng.dma_start(
                out=dst[:, :, :], in_=src), inc="dw", amt=16)
        dw = p.E("gpsimd", lambda eng: eng.dma_start(out=wdt2l[:, :, :], in_=wdt2T),
                 inc="dw", amt=16)
        dcb = p.E("sync", lambda eng: eng.dma_start(out=bdt2[:, :], in_=bdt2c),
                  inc="dx", amt=16)
        vones = p.E("vector", lambda eng: eng.memset(onesT[:, :], 1.0), inc="ve")

        Bm_t = st.enter_context(nc.sbuf_tensor("Bm_t", [NS, TC], F32))
        Cm_t = st.enter_context(nc.sbuf_tensor("Cm_t", [NS, TC], F32))

        hist = {0: {}, 1: {}}
        last_a_sp = 0
        last_dve_dt = 0
        last_abar_read = 0
        rab_hist = [0, 0]
        cb_hist = [0, 0]
        prev_n_pe = 0
        abar_free = [0, 0]
        scum_store = 0
        chunk_done_ve = 0
        ct_stored = 0
        for c in range(NCH):
            par = c % 2
            c0 = c * TC
            h = hist[par]
            xv = p.E("sync", lambda eng, par=par, c0=c0: eng.dma_start(
                out=xs[par][:, :, :], in_=xT_r[:, :, c0:c0 + TC]),
                waits=[("ve", h.get("x_consumed", 0)), ("do", h.get("do", 0))],
                inc="dx", amt=16)
            lnr = emit_layernorm(
                p, [xs[par][:, k, :] for k in range(ND)],
                [xlnb[par][:, k, :] for k in range(ND)], ps, rows,
                sq_waits=[("dx", xv)],
                bc_waits=[("ve", hist.get("dve_xln_prev", 0))],
                xln_waits=[("pe", h.get("pe_pr", 0)), ("ve", h.get("xln_consumed", 0))],
                xln32_tiles=[xlns[par][:, k, :] for k in range(ND)])
            hist["dve_xln_prev"] = lnr["dve_xln"]

            def bt_fn(eng, par=par):
                last = None
                for k in range(ND):
                    last = eng.matmul(ps_m[0][0:NS, :], wbl[:, k, :],
                                      xlnb[par][:, k, :], start=(k == 0), stop=(k == ND - 1))
                return last
            pe_bt = p.E("tensor", bt_fn,
                        waits=[("ve", lnr["dve_xln"]), ("dw", dw),
                               ("ve", chunk_done_ve)],
                        inc="pe")
            ve_bt = p.E("vector", lambda eng: eng.tensor_copy(
                out=bt_sb[:, :], in_=ps_m[0][0:NS, :]),
                waits=[("pe", pe_bt)], inc="ve")

            def ct_fn(eng, par=par):
                last = None
                for k in range(ND):
                    last = eng.matmul(ps_m[1][0:NS, :], wcl[:, k, :],
                                      xlnb[par][:, k, :], start=(k == 0), stop=(k == ND - 1))
                return last
            pe_ct = p.E("tensor", ct_fn, waits=[("ve", chunk_done_ve)], inc="pe")
            ve_ct = p.E("vector", lambda eng: eng.tensor_copy(
                out=ct_sb[:, :], in_=ps_m[1][0:NS, :]),
                waits=[("pe", pe_ct), ("do", ct_stored)], inc="ve")
            ct_stored = p.E("sync", lambda eng, c0=c0: eng.dma_start(
                out=ctd.ap()[:, c0:c0 + TC], in_=ct_sb[:, :]),
                waits=[("ve", ve_ct)], inc="do", amt=16)

            def t1_fn(eng, par=par):
                last = None
                for k in range(ND):
                    last = eng.matmul(ps_m[0][0:DTR, :], wdt1l[:, k, :],
                                      xlnb[par][:, k, :], start=(k == 0), stop=(k == ND - 1))
                return last
            pe_t1 = p.E("tensor", t1_fn, waits=[("ve", ve_bt)], inc="pe")
            a_t1c = p.E("scalar", lambda eng: eng.copy(out=t1b[:, :], in_=ps_m[0][0:DTR, :]),
                        waits=[("pe", pe_t1)], inc="ac")
            h["pe_pr"] = pe_t1

            for j in range(ND):
                mi = j % 2
                pe_dt = p.E("tensor", lambda eng, j=j, mi=mi: eng.matmul(
                    ps_m[mi][:, :], wdt2l[:, j, :], t1b[:, :], start=True, stop=True),
                    waits=[("ac", a_t1c), ("ac", last_a_sp),
                           ("ve", ve_ct if mi == 1 else 0)], inc="pe")

                def sp_fn(eng, j=j, mi=mi):
                    eng.activation(out=l_sp[:, :], in_=ps_m[mi][:, :], func=AF.Abs, bias=bdt2[:, j:j + 1])
                    eng.activation(out=l_sp[:, :], in_=l_sp[:, :], func=AF.Exp, scale=-1.0)
                    eng.activation(out=l_sp[:, :], in_=l_sp[:, :], func=AF.Ln, bias=1.0)
                    return eng.activation(out=r_sp[:, :], in_=ps_m[mi][:, :], func=AF.Relu, bias=bdt2[:, j:j + 1])
                last_a_sp = p.E("scalar", sp_fn,
                                waits=[("pe", pe_dt), ("ve", last_dve_dt), ("dx", dcb)], inc="ac")
                last_dve_dt = p.E("vector", lambda eng, j=j: eng.tensor_add(
                    out=dts[:, j, :], in0=l_sp[:, :], in1=r_sp[:, :]),
                    waits=[("ac", last_a_sp), ("ac", last_abar_read),
                           ("do", scum_store)], inc="ve")
                sc_scan = p.E("vector", lambda eng, j=j: eng.tensor_tensor_scan(
                    out=scums[:, j, :], data0=onesT[:, :], data1=dts[:, j, :],
                    initial=carry_s[:, j:j + 1], op0=ALU.mult, op1=ALU.add),
                    waits=[("ve", vones), ("ve", vpl)], inc="ve")
                last_dve_dt = p.E("vector", lambda eng, j=j: eng.tensor_copy(
                    out=carry_s[:, j:j + 1], in_=scums[:, j, TC - 1:TC]),
                    waits=[("ve", sc_scan)], inc="ve")
            scum_store = p.E("sync", lambda eng, c0=c0: eng.dma_start(
                out=scum_rr[:, :, c0:c0 + TC], in_=scums[:, :, :]),
                waits=[("ve", last_dve_dt)], inc="do", amt=16)

            ymem = p.E("vector", lambda eng, par=par: eng.memset(y_sb[par][:, :, :], 0.0),
                       waits=[("do", h.get("do", 0)), ("ve", vpl)], inc="ve")
            for n in range(NS):
                dve_m = p.E("vector", lambda eng, n=n: (
                    eng.tensor_scalar_mul(out=Bm_t[:, :], in0=bt_sb[:, :], scalar1=eye16[:, n:n + 1]),
                    eng.tensor_scalar_mul(out=Cm_t[:, :], in0=ct_sb[:, :], scalar1=eye16[:, n:n + 1]))[-1],
                    waits=[("pe", prev_n_pe)], inc="ve")
                cbi = n % 2
                pe_cb = p.E("tensor", lambda eng, cbi=cbi: eng.matmul(
                    ps_m[cbi][:, :], ones16[:, :], Cm_t[:, :], start=True, stop=True),
                    waits=[("ve", dve_m), ("ve", cb_hist[cbi]),
                           ("ac", last_a_sp)], inc="pe")
                cb_reads = 0
                for d in range(ND):
                    ri = (n * ND + d) % 2
                    pe_rab = p.E("tensor", lambda eng, ri=ri, d=d: eng.matmul(
                        ps_rab[ri][:, :], SA[:, d * 128:(d + 1) * 128], Bm_t[0:NS, :],
                        start=True, stop=True),
                        waits=[("ve", dve_m), ("ve", rab_hist[ri])], inc="pe")
                    prev_n_pe = pe_rab
                    ai = (n * ND + d) % 2
                    dn = d * NS + n
                    a_ab = p.E("scalar", lambda eng, d=d, dn=dn, ai=ai: eng.activation(
                        out=abar[ai][:, :], in_=dts[:, d, :], func=AF.Exp,
                        scale=A_pn[:, dn:dn + 1]),
                        waits=[("ve", abar_free[ai]), ("ve", last_dve_dt)], inc="ac")
                    last_abar_read = a_ab
                    dve_t2 = p.E("vector", lambda eng, par=par, d=d, ri=ri: eng.tensor_tensor(
                        out=t2[:, :], in0=xlns[par][:, d, :], in1=ps_rab[ri][:, :], op=ALU.mult),
                        waits=[("pe", pe_rab)], inc="ve")
                    rab_hist[ri] = dve_t2
                    p.E("vector", lambda eng, ai=ai: eng.scalar_tensor_tensor(
                        out=b_t[:, :], in0=abar[ai][:, :], scalar=-1.0, in1=t2[:, :],
                        op0=ALU.add, op1=ALU.mult),
                        waits=[("ac", a_ab)], inc="ve")
                    dve_h = p.E("vector", lambda eng, dn=dn, ai=ai: eng.tensor_tensor_scan(
                        out=h_t[:, :], data0=abar[ai][:, :], data1=b_t[:, :],
                        initial=carry[:, dn:dn + 1], op0=ALU.mult, op1=ALU.add),
                        inc="ve")
                    abar_free[ai] = dve_h
                    p.E("vector", lambda eng, dn=dn: eng.tensor_copy(
                        out=carry[:, dn:dn + 1], in_=h_t[:, TC - 1:TC]),
                        waits=[("ve", dve_h)], inc="ve")
                    cb_reads = p.E("vector", lambda eng, cbi=cbi: eng.tensor_tensor(
                        out=tmpy[:, :], in0=h_t[:, :], in1=ps_m[cbi][:, :], op=ALU.mult),
                        waits=[("pe", pe_cb), ("ve", ymem)], inc="ve")
                    p.E("vector", lambda eng, par=par, d=d: eng.tensor_tensor(
                        out=y_sb[par][:, d, :], in0=y_sb[par][:, d, :], in1=tmpy[:, :], op=ALU.add),
                        inc="ve")
                cb_hist[cbi] = cb_reads
            ve_r1 = 0
            for k in range(ND):
                p.E("vector", lambda eng, par=par, k=k: eng.scalar_tensor_tensor(
                    out=y_sb[par][:, k, :], in0=xlns[par][:, k, :], scalar=dsk[:, k:k + 1],
                    in1=y_sb[par][:, k, :], op0=ALU.mult, op1=ALU.add),
                    waits=[("dx", dc)], inc="ve")
                ve_r1 = p.E("vector", lambda eng, par=par, k=k: eng.tensor_add(
                    out=y_sb[par][:, k, :], in0=y_sb[par][:, k, :], in1=xs[par][:, k, :]),
                    inc="ve")
            do = p.E("sync", lambda eng, par=par, c0=c0: eng.dma_start(
                out=r1_rr[:, :, c0:c0 + TC], in_=y_sb[par][:, :, :]),
                waits=[("ve", ve_r1)], inc="do", amt=16)
            h.update(do=do, x_consumed=ve_r1, xln_consumed=ve_r1)
            chunk_done_ve = p.c("ve")

        sfin = p.E("sync", lambda eng: eng.dma_start(
            out=hfin_src.ap(), in_=carry[:, :]),
            waits=[("ve", p.c("ve"))], inc="do", amt=16)
        cc1 = p.E("gpsimd", lambda eng: eng.collective_compute(
            "AllGather", ALU.bypass, replica_groups=PAIRS,
            ins=[hfin_src.ap().opt()], outs=[hfin_g.ap().opt()]),
            waits=[("do", sfin)], inc="cc", amt=1)
        dhin = p.E("sync", lambda eng: eng.dma_start(
            out=h_in[:, :], in_=hfin_g.ap()[0:128, :]),
            waits=[("cc", cc1)], inc="dx", amt=16)
        vhin = p.E("vector", lambda eng: eng.tensor_scalar_mul(
            out=h_in[:, :], in0=h_in[:, :], scalar1=flag_sb[:, :]),
            waits=[("dx", dhin), ("dx", dc)], inc="ve")

    p.phase = "ph3"
    p.barrier(extra_sync_waits=[("do", p.c("do")), ("dw", p.c("dw"))])
    with contextlib.ExitStack() as st:
        wgl_s = [st.enter_context(nc.sbuf_tensor(f"wgl_s{i}", [128, D], BF16)) for i in range(2)]
        wul_s = [st.enter_context(nc.sbuf_tensor(f"wul_s{i}", [128, D], BF16)) for i in range(2)]
        wd_sb = [st.enter_context(nc.sbuf_tensor(f"wd_sb{i}", [128, H0], BF16)) for i in range(2)]
        r1_sb = st.enter_context(nc.sbuf_tensor("r1_sb", [128, ND, TC], F32))
        scum_sb = st.enter_context(nc.sbuf_tensor("scum_sb", [128, ND, TC], F32))
        cts3 = st.enter_context(nc.sbuf_tensor("cts3", [NS, TC], F32))
        lnb3v = st.enter_context(nc.sbuf_tensor("lnb3v3", [128, ND, TC], BF16))
        hb = st.enter_context(nc.sbuf_tensor("hb", [128, NO0, TC], BF16))
        sg = st.enter_context(nc.sbuf_tensor("sg", [128, TC], F32))
        ebuf = [st.enter_context(nc.sbuf_tensor(f"ebuf{i}", [128, TC], F32)) for i in range(2)]
        e2 = st.enter_context(nc.sbuf_tensor("e2", [128, TC], F32))
        t3 = st.enter_context(nc.sbuf_tensor("t3", [128, TC], F32))
        Cm3 = st.enter_context(nc.sbuf_tensor("Cm3", [NS, TC], F32))
        ps_p = [st.enter_context(nc.psum_tensor(f"ps_p{i}", [128, TC], F32)) for i in range(3)]

        h = {}
        gu_used = [0, 0]
        wd_used = [0, 0]
        cb3_hist = [0, 0]
        ebuf_free = [0, 0]
        prev_pe3 = 0
        for c in range(NCH):
            c0 = c * TC
            dl1 = p.E("sync", lambda eng, c0=c0: eng.dma_start(
                out=r1_sb[:, :, :], in_=r1_rr[:, :, c0:c0 + TC]),
                waits=[("do", h.get("do", 0)), ("ve", h.get("r1_free", 0))], inc="dx", amt=16)
            dl2 = p.E("sync", lambda eng, c0=c0: eng.dma_start(
                out=scum_sb[:, :, :], in_=scum_rr[:, :, c0:c0 + TC]),
                waits=[("ac", h.get("scum_free", 0))], inc="dx", amt=16)
            dl3 = p.E("sync", lambda eng, c0=c0: eng.dma_start(
                out=cts3[:, :], in_=ctd.ap()[:, c0:c0 + TC]),
                waits=[("ve", h.get("ct_free", 0))], inc="dx", amt=16)
            ve_corr = 0
            for n in range(NS):
                dve_m = p.E("vector", lambda eng, n=n: eng.tensor_scalar_mul(
                    out=Cm3[:, :], in0=cts3[:, :], scalar1=eye16[:, n:n + 1]),
                    waits=[("dx", dl3), ("pe", prev_pe3), ("ve", vhin)], inc="ve")
                cbi = n % 2
                pe_cb = p.E("tensor", lambda eng, cbi=cbi: eng.matmul(
                    ps_p[cbi][:, :], ones16[:, :], Cm3[:, :], start=True, stop=True),
                    waits=[("ve", dve_m), ("ve", cb3_hist[cbi]),
                           ("ac", h.get("ac_epi", 0)), ("ve", h.get("dve_h_mult", 0))],
                    inc="pe")
                prev_pe3 = pe_cb
                cb_reads = 0
                for d in range(ND):
                    ai = (n * ND + d) % 2
                    dn = d * NS + n
                    a_e = p.E("scalar", lambda eng, d=d, dn=dn, ai=ai: eng.activation(
                        out=ebuf[ai][:, :], in_=scum_sb[:, d, :], func=AF.Exp,
                        scale=A_pn[:, dn:dn + 1]),
                        waits=[("dx", dl2), ("ve", ebuf_free[ai])], inc="ac")
                    ve_e2 = p.E("vector", lambda eng, dn=dn, ai=ai: eng.tensor_scalar_mul(
                        out=e2[:, :], in0=ebuf[ai][:, :], scalar1=h_in[:, dn:dn + 1]),
                        waits=[("ac", a_e)], inc="ve")
                    ebuf_free[ai] = ve_e2
                    cb_reads = p.E("vector", lambda eng, cbi=cbi: eng.tensor_tensor(
                        out=t3[:, :], in0=e2[:, :], in1=ps_p[cbi][:, :], op=ALU.mult),
                        waits=[("pe", pe_cb), ("dx", dl1)], inc="ve")
                    ve_corr = p.E("vector", lambda eng, d=d: eng.tensor_tensor(
                        out=r1_sb[:, d, :], in0=r1_sb[:, d, :], in1=t3[:, :], op=ALU.add),
                        inc="ve")
                cb3_hist[cbi] = cb_reads
            h["scum_free"] = p.c("ac")
            h["ct_free"] = p.c("ve")
            lnr = emit_layernorm(
                p, [r1_sb[:, k, :] for k in range(ND)],
                [lnb3v[:, k, :] for k in range(ND)], ps, rows,
                sq_waits=[("ve", ve_corr)],
                bc_waits=[("ve", h.get("dve_xln_prev", 0))],
                xln_waits=[("pe", h.get("pe_mm_prev", 0))])
            h["dve_xln_prev"] = lnr["dve_xln"]

            ac_epi = h.get("ac_epi", 0)
            dve_h_mult = h.get("dve_h_mult", 0)
            pe_mm = 0
            for o in range(NO0):
                wp = o % 2
                p.E("gpsimd", lambda eng, o=o, wp=wp: eng.dma_start(
                    out=wgl_s[wp][:, :], in_=wg_l[o, :, :]),
                    waits=[("pe", gu_used[wp])], inc="dw", amt=16)
                dwv = p.E("gpsimd", lambda eng, o=o, wp=wp: eng.dma_start(
                    out=wul_s[wp][:, :], in_=wu_l[o, :, :]),
                    waits=[("pe", gu_used[wp])], inc="dw", amt=16)

                def gu_fn(eng, wp=wp):
                    for k in range(ND):
                        eng.matmul(ps_p[0][:, :], wgl_s[wp][:, k * 128:(k + 1) * 128],
                                   lnb3v[:, k, :], start=(k == 0), stop=(k == ND - 1))
                    last = None
                    for k in range(ND):
                        last = eng.matmul(ps_p[1][:, :], wul_s[wp][:, k * 128:(k + 1) * 128],
                                          lnb3v[:, k, :], start=(k == 0), stop=(k == ND - 1))
                    return last
                pe_mm = p.E("tensor", gu_fn,
                            waits=[("ve", lnr["dve_xln"]), ("dw", dwv), ("ac", ac_epi),
                                   ("ve", dve_h_mult),
                                   ("ve", cb_reads if o == 0 else 0)],
                            inc="pe")
                gu_used[wp] = pe_mm
                ac_epi = p.E("scalar", lambda eng: eng.activation(
                    out=sg[:, :], in_=ps_p[0][:, :], func=AF.Silu, scale=QSCALE),
                    waits=[("pe", pe_mm), ("ve", dve_h_mult)], inc="ac")
                dve_h_mult = p.E("vector", lambda eng, o=o: eng.scalar_tensor_tensor(
                    out=hb[:, o, :], in0=ps_p[1][:, :], scalar=QSCALE, in1=sg[:, :],
                    op0=ALU.mult, op1=ALU.mult),
                    waits=[("ac", ac_epi)], inc="ve")
            h["ac_epi"] = ac_epi
            h["dve_h_mult"] = dve_h_mult
            h["pe_mm_prev"] = pe_mm
            prev_pe3 = pe_mm

            dve_r2 = h.get("dve_r2", 0)
            for j in range(ND):
                wp = j % 2
                wdl = p.E("gpsimd", lambda eng, j=j, wp=wp: eng.dma_start(
                    out=wd_sb[wp][:, :], in_=wd_l[j, :, :]),
                    waits=[("pe", wd_used[wp])], inc="dw", amt=16)

                def dn_fn(eng, wp=wp):
                    last = None
                    for k in range(NO0):
                        last = eng.matmul(ps_p[2][:, :], wd_sb[wp][:, k * 128:(k + 1) * 128],
                                          hb[:, k, :], start=(k == 0), stop=(k == NO0 - 1))
                    return last
                pe_dn = p.E("tensor", dn_fn,
                            waits=[("dw", wdl), ("ve", dve_h_mult), ("ve", dve_r2)],
                            inc="pe")
                wd_used[wp] = pe_dn
                prev_pe3 = pe_dn
                dve_r2 = p.E("vector", lambda eng, j=j: eng.scalar_tensor_tensor(
                    out=r1_sb[:, j, :], in0=ps_p[2][:, :], scalar=QSCALE,
                    in1=r1_sb[:, j, :], op0=ALU.mult, op1=ALU.add),
                    waits=[("pe", pe_dn)], inc="ve")
            h["dve_r2"] = dve_r2
            h["r1_free"] = dve_r2
            do = p.E("sync", lambda eng, c0=c0: eng.dma_start(
                out=r2_rr[:, :, c0:c0 + TC], in_=r1_sb[:, :, :]),
                waits=[("ve", dve_r2)], inc="do", amt=16)
            if c == NCH - 1:
                do = p.E("sync", lambda eng: eng.dma_start(
                    out=halo_src.ap().rearrange("(k p) t -> p k t", p=128),
                    in_=r1_sb[:, :, :]),
                    waits=[("ve", dve_r2)], inc="do", amt=16)
            h["do"] = do

        cc2 = p.E("gpsimd", lambda eng: eng.collective_compute(
            "AllGather", ALU.bypass, replica_groups=PAIRS,
            ins=[halo_src.ap().opt()], outs=[halo_g.ap().opt()]),
            waits=[("do", p.c("do"))], inc="cc", amt=1)

    p.phase = "ph4"
    p.barrier(extra_sync_waits=[("do", p.c("do")), ("dw", p.c("dw")), ("cc", p.c("cc"))])
    with contextlib.ExitStack() as st:
        w_sb = {n: st.enter_context(nc.sbuf_tensor(f"w_{n}", [128, ND, D], BF16)) for n in "qkvo"}
        mask_sb = st.enter_context(nc.sbuf_tensor("mask_sb", [128, 4, 2 * WIN], F32))
        ident = st.enter_context(nc.sbuf_tensor("ident_sb", [128, 128], BF16))
        r2_sb = st.enter_context(nc.sbuf_tensor("r2_sb", [128, ND, TC], F32))
        lnb4v = st.enter_context(nc.sbuf_tensor("lnb4v4", [128, ND, TC], BF16))
        q_sb = st.enter_context(nc.sbuf_tensor("q_sb", [128, ND, TC], BF16))
        k_sb = [st.enter_context(nc.sbuf_tensor(f"k_sb{i}", [128, ND, TC], BF16)) for i in range(2)]
        v_sb = st.enter_context(nc.sbuf_tensor("v_sb", [128, ND, TC], BF16))
        vt_sb = [st.enter_context(nc.sbuf_tensor(f"vt_sb{i}", [128, 4, NH, DH], BF16)) for i in range(2)]
        s_sb2 = [st.enter_context(nc.sbuf_tensor(f"s_sb{i}", [128, 2 * WIN], F32)) for i in range(2)]
        p_sb2 = [st.enter_context(nc.sbuf_tensor(f"p_sb{i}", [128, 2 * WIN], BF16)) for i in range(2)]
        pt_sb2 = [st.enter_context(nc.sbuf_tensor(f"pt_sb{i}", [128, 8, 128], BF16)) for i in range(2)]
        att_sb = st.enter_context(nc.sbuf_tensor("att_sb", [128, ND, TC], BF16))
        mx2 = [st.enter_context(nc.sbuf_tensor(f"mx{i}", [128, 1], F32)) for i in range(2)]
        rsum2 = [st.enter_context(nc.sbuf_tensor(f"rsum{i}", [128, 1], F32)) for i in range(2)]
        ps_pj0 = st.enter_context(nc.psum_tensor("ps_pj0", [128, TC], F32))
        ps_pj = [ps_pj0, ps_pj0]
        ps_tp = st.enter_context(nc.psum_tensor("ps_tp", [128, 1024], BF16))
        ps_s2 = [st.enter_context(nc.psum_tensor(f"ps_s{i}", [128, WIN], F32)) for i in range(2)]

        w_rs = {n: ap.rearrange("(k p) o -> p k o", p=128)
                for n, ap in (("q", wqT), ("k", wkT), ("v", wvT), ("o", woT))}
        dw4 = 0
        for n in "qkvo":
            dw4 = p.E("gpsimd", lambda eng, n=n: eng.dma_start(
                out=w_sb[n][:, :, :], in_=w_rs[n]), inc="dw", amt=16)
        dw4 = p.E("gpsimd", lambda eng: eng.dma_start(out=ident[:, :], in_=ident_d),
                  inc="dw", amt=16)
        h0_dc4 = p.E("sync", lambda eng: eng.dma_start(
            out=mask_sb[:, :, :], in_=mask_0.rearrange("q p k -> p q k")), inc="dx", amt=16)

        h = {"dc4": h0_dc4}

        def emit_chunk(c):
            par = c % 2
            c0 = c * TC
            src = halo_rr if c < 0 else r2_rr[:, :, c0:c0 + TC]
            dl = p.E("sync", lambda eng, src=src: eng.dma_start(out=r2_sb[:, :, :], in_=src),
                     waits=[("ve", h.get("r2_consumed", 0)), ("do", h.get("do", 0))],
                     inc="dx", amt=16)
            lnr = emit_layernorm(
                p, [r2_sb[:, k, :] for k in range(ND)],
                [lnb4v[:, k, :] for k in range(ND)], ps, rows,
                sq_waits=[("dx", dl)],
                bc_waits=[("ve", h.get("dve_xln_prev", 0))],
                xln_waits=[("pe", h.get("pe_qkv_prev", 0))])
            h["dve_xln_prev"] = lnr["dve_xln"]

            pe_qkv = 0
            ac_ev = h.get("ac_ev", 0)
            names = ("k", "v") if c < 0 else ("q", "k", "v")
            dst_map = {"q": q_sb, "k": k_sb[par], "v": v_sb}
            for wn in names:
                for o in range(ND):
                    pj = o % 2
                    pe_qkv = p.E("tensor", lambda eng, wn=wn, o=o, pj=pj: [
                        eng.matmul(ps_pj[pj][:, :], w_sb[wn][:, k, o * 128:(o + 1) * 128],
                                   lnb4v[:, k, :], start=(k == 0), stop=(k == ND - 1))
                        for k in range(ND)][-1],
                        waits=[("ve", lnr["dve_xln"]), ("dw", dw4),
                               ("ac", h.get("ev0", 0)),
                               ("ve", h.get("wo0", 0))],
                        inc="pe")
                    ac_ev = p.E("scalar", lambda eng, wn=wn, o=o, pj=pj, dst=dst_map[wn]: eng.copy(
                        out=dst[:, o, :], in_=ps_pj[pj][:, :]),
                        waits=[("pe", pe_qkv), ("pe", h.get("kv_consumed_pe", 0))], inc="ac")
                    h["ev0"] = ac_ev
            h["pe_qkv_prev"] = pe_qkv

            ve_tp = h.get("ve_tp", 0)
            for hh in range(NH):
                po = (hh % 2) * 64
                ti = hh // 2
                for s in range(4):
                    pe_vt = p.E("tensor", lambda eng, po=po, ti=ti, s=s, hh=hh: eng.transpose(
                        ps_tp[:, hh * 64:(hh + 1) * 64],
                        v_sb[po:po + 64, ti, s * 128:(s + 1) * 128], ident[po:po + 64, po:po + 64]),
                        waits=[("ac", ac_ev), ("ve", ve_tp),
                               ("pe", h.get("vt_consumed_pe", 0))], inc="pe")
                    ve_tp = p.E("vector", lambda eng, par=par, s=s, hh=hh: eng.tensor_copy(
                        out=vt_sb[par][:, s, hh, :], in_=ps_tp[:, hh * 64:(hh + 1) * 64]),
                        waits=[("pe", pe_vt)], inc="ve")
            h["ve_tp"] = ve_tp
            ve_vt_done = ve_tp
            if c < 0:
                h["r2_consumed"] = lnr["dve_xln"]
                return

            iters = [(hh, qc) for hh in range(NH) for qc in range(4)]
            bank_free = h.get("bank_free", [0, 0])
            st_pend = h.setdefault("st_pend", {})
            ve_att = 0
            if c == 1:
                h["dc4"] = p.E("sync", lambda eng: eng.dma_start(
                    out=mask_sb[:, :, :], in_=mask_m.rearrange("q p k -> p q k")),
                    waits=[("ve", h.get("mask_reads", 0))], inc="dx", amt=16)
            mrow_all = mask_sb
            dc4_w = h["dc4"]

            def stageA(i):
                hh, qc = iters[i]
                b = i % 2
                po = (hh % 2) * 64
                ti = hh // 2
                q_ap = q_sb[po:po + 64, ti, qc * 128:(qc + 1) * 128]
                ve_s = 0
                for half in range(2):
                    kb = k_sb[1 - par] if half == 0 else k_sb[par]
                    pe_s = p.E("tensor", lambda eng, q_ap=q_ap, kb=kb, po=po, ti=ti, b=b: eng.matmul(
                        ps_s2[b][:, :], q_ap, kb[po:po + 64, ti, :], start=True, stop=True),
                        waits=[("ve", bank_free[b] if half == 0 else 0),
                               ("ve", ve_s), ("ac", h.get("ev0", 0))],
                        inc="pe")
                    ve_s = p.E("vector", lambda eng, qc=qc, half=half, b=b: eng.scalar_tensor_tensor(
                        out=s_sb2[b][:, half * WIN:(half + 1) * WIN], in0=ps_s2[b][:, :],
                        scalar=0.125, in1=mrow_all[:, qc, half * WIN:(half + 1) * WIN],
                        op0=ALU.mult, op1=ALU.add),
                        waits=[("pe", pe_s), ("dx", dc4_w),
                               ("ac", st_pend.get(("s_free", b), 0))], inc="ve")
                ve_mx = p.E("vector", lambda eng, b=b: eng.tensor_reduce(
                    out=mx2[b][:, :], in_=s_sb2[b][:, :], axis=AX.X, op=ALU.max, negate=True),
                    waits=[("ve", ve_s), ("ac", st_pend.get(("mx_free", b), 0))], inc="ve")
                ac_p = p.E("scalar", lambda eng, b=b: eng.activation(
                    out=p_sb2[b][:, :], in_=s_sb2[b][:, :], func=AF.Exp, bias=mx2[b][:, :]),
                    waits=[("ve", ve_mx), ("ve", st_pend.get(("rsum_free", b), 0)),
                           ("pe", st_pend.get(("p_free", b), 0))], inc="ac")
                st_pend[("ve_s", i % 2, iters[i])] = ve_s
                st_pend[("ac_p", i % 2, iters[i])] = ac_p
                st_pend[("s_free", b)] = ac_p
                st_pend[("mx_free", b)] = ac_p

            def stageB(i):
                nonlocal ve_att
                hh, qc = iters[i]
                b = i % 2
                po = (hh % 2) * 64
                ti = hh // 2
                ve_rs = p.E("vector", lambda eng, b=b: eng.tensor_reduce(
                    out=rsum2[b][:, :], in_=p_sb2[b][:, :], axis=AX.X, op=ALU.add),
                    waits=[("ac", st_pend.pop(("ac_p", b, iters[i])))], inc="ve")
                ve_rc = p.E("vector", lambda eng, b=b: eng.reciprocal(
                    out=rsum2[b][:, :], in_=rsum2[b][:, :]),
                    waits=[("ve", ve_rs)], inc="ve")
                gp_sm = p.E("vector", lambda eng, b=b: eng.tensor_scalar_mul(
                    out=p_sb2[b][:, :], in0=p_sb2[b][:, :], scalar1=rsum2[b][:, :]),
                    waits=[("ve", ve_rc)], inc="ve")
                st_pend[("rsum_free", b)] = gp_sm
                pe_pt = p.E("tensor", lambda eng, b=b: [
                    eng.transpose(ps_tp[:, s * 128:(s + 1) * 128],
                                  p_sb2[b][:, s * 128:(s + 1) * 128], ident[:, :])
                    for s in range(8)][-1],
                    waits=[("ve", gp_sm), ("ve", h.get("ve_tp", 0))], inc="pe")
                st_pend[("p_free", b)] = pe_pt
                ve_ptev = p.E("vector", lambda eng, b=b: eng.tensor_copy(
                    out=pt_sb2[b][:, :, :], in_=ps_tp[:, :]),
                    waits=[("pe", pe_pt)], inc="ve")
                h["ve_tp"] = ve_ptev

                def pv_fn(eng, par=par, hh=hh, b=b):
                    last = None
                    for s in range(8):
                        vts = vt_sb[1 - par] if s < 4 else vt_sb[par]
                        last = eng.matmul(ps_s2[b][0:64, 0:128], vts[:, s % 4, hh, :],
                                          pt_sb2[b][:, s, :], start=(s == 0), stop=(s == 7))
                    return last
                pe_pv = p.E("tensor", pv_fn,
                            waits=[("ve", ve_ptev), ("ve", st_pend.pop(("ve_s", b, iters[i]))),
                                   ("ve", ve_vt_done)], inc="pe")
                h["vt_consumed_pe"] = pe_pv
                ve_att = p.E("vector", lambda eng, po=po, ti=ti, qc=qc, b=b: eng.tensor_copy(
                    out=att_sb[po:po + 64, ti, qc * 128:(qc + 1) * 128], in_=ps_s2[b][0:64, 0:128]),
                    waits=[("pe", pe_pv)], inc="ve")
                bank_free[b] = ve_att

            for i in range(len(iters) + 1):
                if i < len(iters):
                    stageA(i)
                if i > 0:
                    stageB(i - 1)
            h["bank_free"] = bank_free
            h["mask_reads"] = p.c("ve")

            dve_r3 = h.get("dve_r3", 0)
            for o in range(ND):
                pj = o % 2
                pe_wo = p.E("tensor", lambda eng, o=o, pj=pj: [
                    eng.matmul(ps_pj[pj][:, :], w_sb["o"][:, k, o * 128:(o + 1) * 128],
                               att_sb[:, k, :], start=(k == 0), stop=(k == ND - 1))
                    for k in range(ND)][-1],
                    waits=[("ve", ve_att), ("ve", h.get("wo0", 0))], inc="pe")
                h["kv_consumed_pe"] = pe_wo
                dve_r3 = p.E("vector", lambda eng, o=o, pj=pj: eng.tensor_tensor(
                    out=r2_sb[:, o, :], in0=ps_pj[pj][:, :], in1=r2_sb[:, o, :], op=ALU.add),
                    waits=[("pe", pe_wo)], inc="ve")
                h["wo0"] = dve_r3
            h["dve_r3"] = dve_r3
            do = p.E("sync", lambda eng, c0=c0: eng.dma_start(
                out=r3_rr[:, :, c0:c0 + TC], in_=r2_sb[:, :, :]),
                waits=[("ve", dve_r3)], inc="do", amt=16)
            h["do"] = do
            h["r2_consumed"] = dve_r3

        emit_chunk(-1)
        for c in range(NCH):
            emit_chunk(c)

    p.phase = "ph5"
    p.barrier(extra_sync_waits=[("do", p.c("do")), ("dw", p.c("dw"))])
    with contextlib.ExitStack() as st:
        sc_sb = {n: st.enter_context(nc.sbuf_tensor(
            "sc_" + n, [128, NO1 if n != "s3c" else ND], F32)) for n in scd}
        r3_sb = st.enter_context(nc.sbuf_tensor("r3_sb", [128, ND, TCE], F32))
        lnb5v = st.enter_context(nc.sbuf_tensor("lnb5v5", [128, ND, TCE], BF16))
        hbuf = st.enter_context(nc.sbuf_tensor("hbuf", [128, NO1, TCE], BF16))
        gbuf = st.enter_context(nc.sbuf_tensor("gbuf", [128, NO1, TCE], BF16))
        ws = [st.enter_context(nc.sbuf_tensor(f"ws{i}", [128, H1 // 2], BF16)) for i in range(2)]
        t1f = st.enter_context(nc.sbuf_tensor("t1f", [128, TCE], BF16))
        ps_a = st.enter_context(nc.psum_tensor("ps_a", [128, TCE], F32))
        ps_b = st.enter_context(nc.psum_tensor("ps_b", [128, TCE], F32))
        ps_ab = [ps_a, ps_b]

        dc5 = 0
        for n in scd:
            dc5 = p.E("sync", lambda eng, n=n: eng.dma_start(out=sc_sb[n][:, :], in_=scd[n]),
                      inc="dx", amt=16)

        h = {}
        ws_used = [0, 0]
        bank_rd_ac = [0, 0]
        bank_rd_ve = [0, 0]
        for c in range(NCHE):
            c0 = c * TCE
            dl = p.E("sync", lambda eng, c0=c0: eng.dma_start(
                out=r3_sb[:, :, :], in_=r3_rr[:, :, c0:c0 + TCE]),
                waits=[("do", h.get("do", 0)), ("ve", h.get("r3_free", 0))], inc="dx", amt=16)
            xln_done = 0
            for sub in range(2):
                lnr = emit_layernorm(
                    p, [r3_sb[:, k, sub * TC:(sub + 1) * TC] for k in range(ND)],
                    [lnb5v[:, k, sub * TC:(sub + 1) * TC] for k in range(ND)], ps, rows,
                    sq_waits=[("dx", dl)],
                    bc_waits=[("ve", h.get("dve_xln_prev", 0))],
                    xln_waits=[("pe", h.get("pe_mm_prev", 0))])
                h["dve_xln_prev"] = lnr["dve_xln"]
                xln_done = lnr["dve_xln"]

            ve_hb = h.get("ve_hb", 0)
            for o in range(NO1):
                q = o % 2
                p.E("gpsimd", lambda eng, o=o, q=q: eng.dma_start(
                    out=ws[q][:, 0:D], in_=w1_l[o, :, :]),
                    waits=[("pe", ws_used[q])], inc="dwE", amt=16)
                dwv = p.E("gpsimd", lambda eng, o=o, q=q: eng.dma_start(
                    out=ws[q][:, D:2 * D], in_=w2_l[o, :, :]),
                    waits=[("pe", ws_used[q])], inc="dwE", amt=16)

                def mm12(eng, q=q):
                    for k in range(ND):
                        for sub in range(2):
                            eng.matmul(ps_a[:, sub * TC:(sub + 1) * TC],
                                       ws[q][:, k * 128:(k + 1) * 128],
                                       lnb5v[:, k, sub * TC:(sub + 1) * TC],
                                       start=(k == 0), stop=(k == ND - 1))
                    last = None
                    for k in range(ND):
                        for sub in range(2):
                            last = eng.matmul(ps_b[:, sub * TC:(sub + 1) * TC],
                                              ws[q][:, D + k * 128:D + (k + 1) * 128],
                                              lnb5v[:, k, sub * TC:(sub + 1) * TC],
                                              start=(k == 0), stop=(k == ND - 1))
                    return last
                pe_mm = p.E("tensor", mm12,
                            waits=[("dwE", dwv), ("ve", xln_done),
                                   ("ac", max(bank_rd_ac[0], bank_rd_ac[1])),
                                   ("ve", max(bank_rd_ve[0], bank_rd_ve[1], ve_hb))],
                            inc="pe")
                ws_used[q] = pe_mm
                ac_epi = p.E("scalar", lambda eng, o=o: eng.activation(
                    out=t1f[:, :], in_=ps_a[:, :], func=AF.Silu,
                    scale=sc_sb["s1c"][:, o:o + 1]),
                    waits=[("pe", pe_mm), ("dx", dc5), ("ve", ve_hb)], inc="ac")
                bank_rd_ac[0] = ac_epi
                ve_hb = p.E("vector", lambda eng, o=o: eng.scalar_tensor_tensor(
                    out=hbuf[:, o, :], in0=ps_b[:, :], scalar=sc_sb["s2c"][:, o:o + 1],
                    in1=t1f[:, :], op0=ALU.mult, op1=ALU.mult),
                    waits=[("ac", ac_epi), ("pe", h.get("w3_last_pe", 0))], inc="ve")
                bank_rd_ve[1] = ve_hb
            h["ve_hb"] = ve_hb

            def big_layer(w_dram, rhs_buf, dst_buf, scol, func,
                          rhs_ready_ve, rhs_ready_ac, epi="act"):
                nonlocal ws_used, bank_rd_ac, bank_rd_ve
                last_epi_ac = 0
                last_epi_ve = 0
                nouts = NO1 if dst_buf is not None else ND
                nouts = w_dram.shape[0]
                for o in range(nouts):
                    bk = o % 2
                    dmas = []
                    for hf in range(2):
                        q = (o * 2 + hf) % 2
                        dmas.append(p.E("gpsimd", lambda eng, o=o, hf=hf, q=q, w_dram=w_dram: eng.dma_start(
                            out=ws[q][:, :], in_=w_dram[o, :, hf * 2048:(hf + 1) * 2048]),
                            waits=[("pe", ws_used[q])], inc="dwE", amt=16))
                    for hf in range(2):
                        q = (o * 2 + hf) % 2

                        def mmh(eng, hf=hf, q=q, bk=bk, rhs_buf=rhs_buf):
                            last = None
                            for kk in range(16):
                                k = hf * 16 + kk
                                for sub in range(2):
                                    last = eng.matmul(ps_ab[bk][:, sub * TC:(sub + 1) * TC],
                                                      ws[q][:, kk * 128:(kk + 1) * 128],
                                                      rhs_buf[:, k, sub * TC:(sub + 1) * TC],
                                                      start=(k == 0), stop=(k == NO1 - 1))
                            return last
                        pe_f = p.E("tensor", mmh,
                                   waits=[("dwE", dmas[hf]), ("ve", rhs_ready_ve),
                                          ("ac", max(rhs_ready_ac, bank_rd_ac[bk] if hf == 0 else 0)),
                                          ("ve", bank_rd_ve[bk] if hf == 0 else 0)],
                                   inc="pe")
                        ws_used[q] = pe_f
                    if epi == "act":
                        last_epi_ac = p.E("scalar", lambda eng, o=o, bk=bk, dst_buf=dst_buf: eng.activation(
                            out=dst_buf[:, o, :], in_=ps_ab[bk][:, :], func=func,
                            scale=sc_sb[scol][:, o:o + 1]),
                            waits=[("pe", pe_f)], inc="ac")
                        bank_rd_ac[bk] = last_epi_ac
                    else:
                        last_epi_ve = p.E("vector", lambda eng, o=o, bk=bk: eng.scalar_tensor_tensor(
                            out=r3_sb[:, o, :], in0=ps_ab[bk][:, :], scalar=sc_sb[scol][:, o:o + 1],
                            in1=r3_sb[:, o, :], op0=ALU.mult, op1=ALU.add),
                            waits=[("pe", pe_f)], inc="ve")
                        bank_rd_ve[bk] = last_epi_ve
                return last_epi_ac, last_epi_ve

            ac_g1, _ = big_layer(h1_l, hbuf, gbuf, "sh1c", AF.Gelu, ve_hb, 0)
            ac_g2, _ = big_layer(h2_l, gbuf, hbuf, "sh2c", AF.Gelu, 0, ac_g1)
            _, dve_out = big_layer(w3_l, hbuf, None, "s3c", None, 0, ac_g2, epi="res")

            h["w3_last_pe"] = p.c("pe")
            h["dve_out"] = dve_out
            do = p.E("sync", lambda eng, c0=c0: eng.dma_start(
                out=out_rr[:, :, c0:c0 + TCE], in_=r3_sb[:, :, :]),
                waits=[("ve", dve_out)], inc="do", amt=16)
            h["do"] = do
            h["r3_free"] = dve_out
            h["pe_mm_prev"] = p.c("pe")

    p.E("sync", None, waits=[("do", p.c("do"))])
    p.emit()
    return nc


def _lhsT_stream(W, o_blk=128):
    W = np.ascontiguousarray(W)
    O, I = W.shape
    nj = O // o_blk
    return np.ascontiguousarray(
        W.reshape(nj, o_blk, I // 128, 128).transpose(0, 3, 2, 1)).reshape(nj, 128, (I // 128) * o_blk)


def _build_masks():
    i = np.arange(WIN)[:, None]
    j = np.arange(2 * WIN)[None, :]
    base = (j > i) & (j <= i + WIN)
    m_main = np.where(base, 0.0, -1e30).astype(np.float32)
    m_blk0 = np.where(base & (j >= WIN), 0.0, -1e30).astype(np.float32)
    return (np.ascontiguousarray(m_main.reshape(4, 128, 2 * WIN)),
            np.ascontiguousarray(m_blk0.reshape(4, 128, 2 * WIN)))


_NC_CACHE = {}


def kernel(**inp):
    f32 = np.float32
    cores = list(range(NCORES))
    x_fm = np.ascontiguousarray(np.asarray(inp["L0_attn_x"], f32).transpose(0, 2, 1))
    ones_col = np.ones((128, 1), f32)
    ones_row = np.ones((1, 128), f32)
    m_main, m_blk0 = _build_masks()
    logA = np.asarray(inp["L0_attn_log_A"], f32)

    shared = {
        "wbT": np.ascontiguousarray(np.asarray(inp["L0_attn_W_B"], f32).T),
        "wcT": np.ascontiguousarray(np.asarray(inp["L0_attn_W_C"], f32).T),
        "wdt1T": np.ascontiguousarray(np.asarray(inp["L0_attn_W_dt1"], f32).T),
        "wdt2T": np.ascontiguousarray(np.asarray(inp["L0_attn_W_dt2"], f32).T).reshape(DTR, ND, 128),
        "bdt2c": np.ascontiguousarray(np.asarray(inp["L0_attn_b_dt2"], f32).reshape(ND, 128).T),
        "logA_pn": np.ascontiguousarray(
            logA.reshape(ND, 128, NS).transpose(1, 0, 2)).reshape(128, ND * NS),
        "logAT": np.ascontiguousarray(logA.T),
        "eye16": np.eye(NS, dtype=f32),
        "dskc": np.ascontiguousarray(np.asarray(inp["L0_attn_D_skip"], f32).reshape(ND, 128).T),
        "wg_l": _lhsT_stream(np.asarray(inp["L0_ffn_w_gate"])),
        "wu_l": _lhsT_stream(np.asarray(inp["L0_ffn_w_up"])),
        "wd_l": _lhsT_stream(np.asarray(inp["L0_ffn_w_down"])),
        "wqT": np.ascontiguousarray(np.asarray(inp["L1_attn_Wq"], f32).T),
        "wkT": np.ascontiguousarray(np.asarray(inp["L1_attn_Wk"], f32).T),
        "wvT": np.ascontiguousarray(np.asarray(inp["L1_attn_Wv"], f32).T),
        "woT": np.ascontiguousarray(np.asarray(inp["L1_attn_Wo"], f32).T),
        "mask_m": m_main,
        "ident": np.eye(128).astype(ml_dtypes.bfloat16),
        "w1_l": _lhsT_stream(np.asarray(inp["L1_ffn_w1_w"], f32)),
        "w2_l": _lhsT_stream(np.asarray(inp["L1_ffn_w2_w"], f32)),
        "h1_l": _lhsT_stream(np.asarray(inp["L1_ffn_h1_w"], f32)),
        "h2_l": _lhsT_stream(np.asarray(inp["L1_ffn_h2_w"], f32)),
        "w3_l": _lhsT_stream(np.asarray(inp["L1_ffn_w3_w"], f32)),
        "s1c": np.ascontiguousarray(np.asarray(inp["L1_ffn_w1_s"], f32).reshape(NO1, 128).T),
        "s2c": np.ascontiguousarray(np.asarray(inp["L1_ffn_w2_s"], f32).reshape(NO1, 128).T),
        "sh1c": np.ascontiguousarray(np.asarray(inp["L1_ffn_h1_s"], f32).reshape(NO1, 128).T),
        "sh2c": np.ascontiguousarray(np.asarray(inp["L1_ffn_h2_s"], f32).reshape(NO1, 128).T),
        "s3c": np.ascontiguousarray(np.asarray(inp["L1_ffn_w3_s"], f32).reshape(ND, 128).T),
        "ones_col": ones_col,
        "ones_row": ones_row,
    }

    ims = []
    for c in cores:
        b, hf = c // 2, c % 2
        im = dict(shared)
        im["xT"] = np.ascontiguousarray(x_fm[b, :, hf * TPC:(hf + 1) * TPC])
        im["flag"] = np.full((128, 1), float(hf), f32)
        im["mask_0"] = m_blk0 if hf == 0 else m_main
        ims.append(im)

    nc = build_fused()
    res = run_bass_kernel_spmd(nc, ims, core_ids=cores)
    out = np.empty((B, T, D), f32)
    for c in cores:
        b, hf = c // 2, c % 2
        out[b, hf * TPC:(hf + 1) * TPC, :] = res.results[c]["outT"].T
    return out
